# version 09
# baseline (speedup 1.0000x reference)
import numpy as np
from contextlib import ExitStack
import concourse.bass as bass
import concourse.mybir as mybir
from concourse.bass_utils import run_bass_kernel_spmd

F32 = mybir.dt.float32
BF16 = mybir.dt.bfloat16
I32 = mybir.dt.int32
AF = mybir.ActivationFunctionType
ALU = mybir.AluOpType
AX = mybir.AxisListType

TOK = 2048
C = 2048
H = 8192
CJ = 16
NT = 4
NCK = 16
HCN = 64
INV_C = 1.0 / 2048.0
FULL_K = True

ENGS = ("g", "s", "t", "v", "c")


class _Sem:
    __slots__ = ("name", "n", "h")

    def __init__(self, name):
        self.name, self.n, self.h = name, 0, None


class Sched:
    def __init__(self):
        self.streams = {e: [] for e in ENGS}
        self.sems = []
        self.wm = {e: {} for e in ENGS}

    def sem(self, name):
        s = _Sem(name)
        self.sems.append(s)
        return s

    def step(self, eng, fn, waits=(), inc=None, n=1):
        w = []
        for item in waits:
            if item is None:
                continue
            sem, val = item
            if val is None or val <= 0:
                continue
            if self.wm[eng].get(sem.name, 0) >= val:
                continue
            self.wm[eng][sem.name] = val
            w.append((sem, val))
        ev = None
        if inc is not None:
            inc.n += n
            ev = (inc, inc.n)
        self.streams[eng].append((w, fn, inc, n))
        return ev

    def check_deadlock(self):
        ptr = {e: 0 for e in ENGS}
        val = {s.name: 0 for s in self.sems}
        progress = True
        while progress:
            progress = False
            for e in ENGS:
                while ptr[e] < len(self.streams[e]):
                    w, fn, inc, n = self.streams[e][ptr[e]]
                    if any(val[s.name] < v for s, v in w):
                        break
                    if inc is not None:
                        val[inc.name] += n
                    ptr[e] += 1
                    progress = True
        stuck = {e: (ptr[e], len(self.streams[e])) for e in ENGS
                 if ptr[e] < len(self.streams[e])}
        if stuck:
            det = []
            for e, (p, tot) in stuck.items():
                w = self.streams[e][p][0]
                bad = [(s.name, v, val[s.name]) for s, v in w if val[s.name] < v]
                det.append(f"{e}: step {p}/{tot} waits {bad}")
            raise RuntimeError("schedule deadlock: " + "; ".join(det))


def build():
    nc = bass.Bass("TRN2", target_bir_lowering=False)

    x_d = nc.dram_tensor("x", [TOK, C], F32, kind="ExternalInput").ap()
    wm_d = [nc.dram_tensor(nm, [C, C], F32, kind="ExternalInput").ap()
            for nm in ("w_rec", "w_value", "w_key")]
    wo_d = nc.dram_tensor("w_out", [C, C], F32, kind="ExternalInput").ap()
    vecs = {nm: nc.dram_tensor(nm, [C], F32, kind="ExternalInput").ap()
            for nm in ("tf", "pp", "aa", "bb", "tmr", "tmv", "tmk", "xx")}
    upq_d = nc.dram_tensor("w_up_q", [H, C], I32, kind="ExternalInput").ap()
    upsc_d = nc.dram_tensor("w_up_scale", [H, 32], F32, kind="ExternalInput").ap()
    bup_d = nc.dram_tensor("b_up", [H], F32, kind="ExternalInput").ap()
    upA_d = nc.dram_tensor("up_lora_a", [C, 16], F32, kind="ExternalInput").ap()
    upB_d = nc.dram_tensor("up_lora_b", [16, H], F32, kind="ExternalInput").ap()
    dnq_d = nc.dram_tensor("w_down_q", [C, H], I32, kind="ExternalInput").ap()
    dnsc_d = nc.dram_tensor("w_down_scale", [C, 128], F32, kind="ExternalInput").ap()
    bdn_d = nc.dram_tensor("b_down", [C], F32, kind="ExternalInput").ap()
    dnA_d = nc.dram_tensor("dn_lora_a", [H, 16], F32, kind="ExternalInput").ap()
    dnB_d = nc.dram_tensor("dn_lora_b", [16, C], F32, kind="ExternalInput").ap()
    out_d = nc.dram_tensor("out", [TOK, C], F32, kind="ExternalOutput").ap()
    upscr_d = nc.dram_tensor("up_scr", [H, 2048], BF16).ap()
    dnscr_d = nc.dram_tensor("dn_scr", [H, C], BF16).ap()
    yscr_d = nc.dram_tensor("y_scr", [TOK, C], F32).ap()
    hscr_d = nc.dram_tensor("h_scr", [TOK, C], F32).ap()

    es = ExitStack()

    SBB = 16512

    def sb(name, shape, dtype, off):
        return nc.alloc_sbuf_tensor_at(name, shape, dtype, offset=SBB + off)

    A = sb("A", [128, 16, 2048], BF16, 0)
    B = sb("B", [128, 32768], BF16, 65536)
    CW = 18816
    Ct = sb("Cx", [128, CW], BF16, 131072)
    DBASE = 131072 + 37632
    qin = [sb(f"qin{i}", [128, 2048], I32, DBASE + i * 8192) for i in range(2)]
    dqo = [sb(f"dqo{i}", [128, 2048], BF16, DBASE + 16384 + i * 4096) for i in range(2)]
    yln = [sb(f"yln{i}", [128, 2048], F32, DBASE + i * 8192) for i in range(2)]
    EBASE = DBASE + 24576
    E = [sb(f"Ee{i}", [128, 2048], F32, EBASE + i * 8192) for i in range(2)]
    FBASE = EBASE + 16384
    fo = [FBASE]

    def fsb(name, shape, dtype):
        t = sb(name, shape, dtype, fo[0])
        width = 1
        for d in shape[1:]:
            width *= d
        fo[0] += (width * mybir.dt.size(dtype) + 31) // 32 * 32
        assert SBB + fo[0] <= 229344, f"F overflow at {name}: {fo[0]}"
        return t

    par = {nm: fsb("p_" + nm, [128, 16], F32) for nm in
           ("tf", "pp", "aa", "bb", "tmr", "tmv", "tmk", "xx")}
    nbias = sb("nbias", [128, 32], F32, fo[0])
    cmf = [fsb(f"cmf{m}", [128, 16], F32) for m in range(3)]
    cmb = [fsb(f"cmb{m}", [128, 16], BF16) for m in range(3)]
    biasSB = [fsb(f"biasSB{m}", [128, 16], F32) for m in range(3)]
    bup_t = fsb("bup", [128, 64], F32)
    upsct = [fsb(f"upsct{i}", [128, 32], F32) for i in range(2)]
    dnsct = [fsb(f"dnsct{i}", [128, 32], F32) for i in range(2)]
    epst = fsb("epst", [128, 1], F32)
    k0t = fsb("k0t", [128, 16], F32)
    v0t = fsb("v0t", [128, 16], F32)
    s0t = fsb("s0t", [128, 16], F32)
    wwt = fsb("wwt", [128, 16], F32)
    qqt = fsb("qqt", [128, 16], F32)
    t1t = fsb("t1t", [128, 16], F32)
    t2t = fsb("t2t", [128, 16], F32)
    e1t = fsb("e1t", [128, 16], F32)
    e2t = fsb("e2t", [128, 16], F32)
    numt = fsb("numt", [128, 16], F32)
    dent = fsb("dent", [128, 16], F32)
    wk0t = fsb("wk0t", [128, 16], F32)
    lnT = [[fsb(f"ln{n}{i}", [128, 1], F32) for n in
            ("sy", "sq", "mn", "vr", "nv", "sd", "rs")] for i in range(2)]
    lnS4 = [fsb(f"lns4{i}", [128, 4], F32) for i in range(2)]

    slab_col = lambda m, p: (m * 2 + p) * 2048
    natW_col = [12288, 14336]
    upstg_col = 16384
    natWo_col = 16384
    woq_col = [0, 8192]
    wdt_col = [0, 2048]
    bupB_col = 4096
    bdnB_col = 12288
    adn_col = 14336
    aup_col = 15360
    bdnv_col = 15616
    ones_col = 17664
    tup_col = 17792
    tdn_col = 18304

    P = [es.enter_context(nc.psum_tensor(f"pb{i}", [128, 512], F32)) for i in range(8)]

    S = Sched()
    st = S.sem("st")
    sv = S.sem("sv")
    sc = S.sem("sc")
    g_par = S.sem("g_par")
    g_wo = S.sem("g_wo")
    g_lo = S.sem("g_lo")
    g_dq = [S.sem(f"g_dq{i}") for i in range(2)]
    g_qi = [S.sem(f"g_qi{i}") for i in range(2)]
    g_sc = [S.sem(f"g_sc{i}") for i in range(2)]
    g_nw = [S.sem(f"g_nw{i}") for i in range(2)]
    g_E = [S.sem(f"g_E{i}") for i in range(2)]
    g_yl = [S.sem(f"g_yl{i}") for i in range(2)]
    s_x = [S.sem(f"s_x{i}") for i in range(2)]
    s_sl = [S.sem(f"s_sl{i}") for i in range(6)]
    s_ut = [S.sem(f"s_ut{i}") for i in range(2)]
    s_stb = [S.sem(f"s_stb{i}") for i in range(2)]
    s_qt = S.sem("s_qt")
    s_qtb = S.sem("s_qtb")
    s_y = [S.sem(f"s_y{i}") for i in range(2)]
    s_out = [S.sem(f"s_out{i}") for i in range(2)]
    s_hst = [S.sem(f"s_hst{i}") for i in range(2)]
    s_hT = [S.sem(f"s_hT{i}") for i in range(2)]
    s_wd = [S.sem(f"s_wd{i}") for i in range(2)]
    for sem in S.sems:
        sem.h = es.enter_context(nc.semaphore(sem.name))

    def tot(*sems):
        return [(s, s.n) for s in sems]

    def ap2(tens, width, off_el, w):
        return bass.AP(tens, off_el, [[width, 128]] + [[1, w]])

    def ap3(tens, width, off_el, dims):
        return bass.AP(tens, off_el, [[width, 128]] + dims)

    def Asl(j, a, w):
        return bass.AP(A, j * 2048 + a, [[32768, 128], [1, w]])

    def ld(dst_ap, src_ap, slow=False):
        def fn(g, d=dst_ap, s_=src_ap, sl=slow):
            if sl:
                return g.dma_start(out=d, in_=s_, allow_slow_non_contiguous=True)
            return g.dma_start(out=d, in_=s_)
        return fn

    ev_par = None
    for nm, t in par.items():
        src = bass.AP(vecs[nm].tensor, 0, [[1, 128], [128, 16]])
        ev_par = S.step("g", ld(t[:, :], src, slow=True), inc=g_par, n=16)
    ev_par = S.step("g", ld(bup_t[:, :],
                            bass.AP(bup_d.tensor, 0, [[1, 128], [128, 64]]), slow=True),
                    inc=g_par, n=16)
    S.step("v", lambda v: v.memset(epst[:, :], 1e-5), inc=sv)
    tm_names = ("tmr", "tmv", "tmk")
    ev_cmb = [None] * 3
    for m in range(3):
        tmt = par[tm_names[m]]
        ev = S.step("v", lambda v, m=m, tmt=tmt: v.tensor_scalar(
            cmf[m][:, :], tmt[:, :], -1.0, 1.0, ALU.mult, op1=ALU.add),
            waits=[ev_par], inc=sv)
        ev = S.step("v", lambda v, m=m: v.tensor_mul(
            cmf[m][:, :], cmf[m][:, :], par["xx"][:, :]),
            waits=[(sv, sv.n)], inc=sv)
        ev_cmb[m] = S.step("v", lambda v, m=m: v.tensor_copy(
            cmb[m][:, :], cmf[m][:, :]), waits=[(sv, sv.n)], inc=sv)

    ev_xtr = [None] * 16
    dq_free = [None, None]
    for tt in range(16):
        ev_l = S.step("g", ld(dqo[tt % 2][:, :], x_d[tt * 128:(tt + 1) * 128, :]),
                      waits=[dq_free[tt % 2]], inc=g_dq[tt % 2], n=16)
        ev_xtr[tt] = S.step("s", lambda s_, tt=tt: s_.dma_start(
            out=A[:, :, tt * 128:(tt + 1) * 128], in_=dqo[tt % 2][:, :], transpose=True),
            waits=[ev_l], inc=s_x[tt % 2], n=16)
        dq_free[tt % 2] = ev_xtr[tt]
    ev_xtr_all = tot(s_x[0], s_x[1])

    prep_units = [("u", hc) for hc in range(HCN)]
    for q in range(4):
        for dc in range(16):
            prep_units.append(("d", dc, q))
    qin_free = [None, None]
    upscr_done = [None] * HCN
    dn_qdone = [None] * 4
    last_upstg = [None]
    _pi = [0]

    def emit_deq(b, sct, evq, evs):
        ev_nb = S.step("c", lambda c_, sct=sct: c_.mul(nbias[:, :], sct[:, :], -7.5),
                       waits=[evs, ev_cmb[2]], inc=sc)

        def facts(c_, b=b, sct=sct):
            last = None
            for blk in range(32):
                last = c_.activation(
                    dqo[b][:, blk * 64:(blk + 1) * 64],
                    qin[b][:, blk * 64:(blk + 1) * 64], AF.Identity,
                    bias=nbias[:, blk:blk + 1], scale=sct[:, blk:blk + 1])
            return last
        return S.step("c", facts, waits=[evq, ev_nb, dq_free[b]], inc=sc)

    def emit_prep_unit():
        i = _pi[0]
        if i >= len(prep_units):
            return
        _pi[0] += 1
        u = prep_units[i]
        b = i % 2
        if u[0] == "u":
            hc = u[1]
            evq = S.step("c", ld(qin[b][:, :], upq_d[hc * 128:(hc + 1) * 128, :]),
                         waits=[qin_free[b]], inc=g_qi[b], n=16)
            evs = S.step("c", ld(upsct[b][:, :], upsc_d[hc * 128:(hc + 1) * 128, :]),
                         inc=g_sc[b], n=16)
            evd = emit_deq(b, upsct[b], evq, evs)
            qin_free[b] = evd
            dst = ap3(Ct, CW, upstg_col, [[128, 16], [1, 128]])
            evtp = S.step("s", lambda s_, b=b, dst=dst: s_.dma_start(
                out=dst, in_=dqo[b][:, :], transpose=True), waits=[evd],
                inc=s_ut[b], n=16)
            evst = S.step("s", lambda s_, hc=hc: s_.dma_start(
                out=upscr_d[hc * 128:(hc + 1) * 128, :],
                in_=Ct[:, upstg_col:upstg_col + 2048]), waits=[evtp],
                inc=s_stb[b], n=16)
            dq_free[b] = evtp
            upscr_done[hc] = evst
            last_upstg[0] = evst
        else:
            dc, q = u[1], u[2]
            evq = S.step("c", ld(qin[b][:, :],
                                 dnq_d[dc * 128:(dc + 1) * 128, q * 2048:(q + 1) * 2048]),
                         waits=[qin_free[b]], inc=g_qi[b], n=16)
            evs = S.step("c", ld(dnsct[b][:, :],
                                 dnsc_d[dc * 128:(dc + 1) * 128, q * 32:(q + 1) * 32]),
                         inc=g_sc[b], n=16)
            evd = emit_deq(b, dnsct[b], evq, evs)
            qin_free[b] = evd
            dst = ap3(Ct, CW, upstg_col, [[128, 16], [1, 128]])
            evtp = S.step("s", lambda s_, b=b, dst=dst: s_.dma_start(
                out=dst, in_=dqo[b][:, :], transpose=True), waits=[evd],
                inc=s_ut[b], n=16)
            ddst = bass.AP(dnscr_d.tensor, q * 2048 * 2048 + dc * 128,
                           [[2048, 128], [128 * 2048, 16], [1, 128]])
            evst = S.step("s", lambda s_, ddst=ddst: s_.dma_start(
                out=ddst, in_=Ct[:, upstg_col:upstg_col + 2048]),
                waits=[evtp], inc=s_stb[b], n=16)
            dq_free[b] = evtp
            dn_qdone[q] = evst

    natW_free = [None, None]
    slab_free = [[None, None] for _ in range(3)]
    bias_bank_free = [None, None]
    sig_free = [None, None, None, None]
    bankR_free = [None, None]
    bankV_free = [None, None]
    bankK_free = [None]
    rwkv_last = None
    pR, pV, pK, pBias = [P[0], P[1]], [P[2], P[3]], [P[4], P[5]], [P[6], P[7]]

    ev_tr_k = {}
    ev_bias_k = {}
    ev_scale_k = {}

    def stage_prep(oc):
        ev_tr = [None] * 3
        for m in range(3):
            nb = (oc * 3 + m) % 2
            evw = S.step("g", ld(Ct[:, natW_col[nb]:natW_col[nb] + 2048],
                                 wm_d[m][oc * 128:(oc + 1) * 128, :]),
                         waits=[natW_free[nb]], inc=g_nw[nb], n=16)
            dst = ap3(Ct, CW, slab_col(m, oc % 2), [[128, 16], [1, 128]])
            ev_tr[m] = S.step("s", lambda s_, nb=nb, dst=dst: s_.dma_start(
                out=dst, in_=Ct[:, natW_col[nb]:natW_col[nb] + 2048], transpose=True),
                waits=[evw, slab_free[m][oc % 2]], inc=s_sl[m * 2 + oc % 2], n=16)
            natW_free[nb] = ev_tr[m]
        ev_tr_k[oc] = ev_tr

    def stage_bias(oc):
        ev_tr = ev_tr_k[oc]
        ev_biasg = [None] * 3
        for m in range(3):
            def fb(t, m=m, oc=oc):
                base = slab_col(m, oc % 2)
                for j in range(CJ):
                    last = t.matmul(pBias[oc % 2][:, m:m + 1],
                                    Ct[:, base + j * 128:base + j * 128 + 128],
                                    cmb[m][:, j:j + 1],
                                    start=(j == 0), stop=(j == CJ - 1))
                return last
            ev_biasg[m] = S.step("t", fb,
                                 waits=[ev_tr[m], ev_cmb[m], bias_bank_free[oc % 2]],
                                 inc=st)
        ev_bias_k[oc] = ev_biasg

    def stage_scale(oc):
        ev_biasg = ev_bias_k[oc]

        def fbe(v, oc=oc):
            last = None
            for m in range(3):
                last = v.tensor_copy(biasSB[m][:, oc:oc + 1], pBias[oc % 2][:, m:m + 1])
            return last
        ev_bev = S.step("v", fbe, waits=[ev_biasg[2]], inc=sv)
        bias_bank_free[oc % 2] = ev_bev
        ev_scale = [None] * 3
        for m in range(3):
            def fs(v, m=m, oc=oc):
                base = slab_col(m, oc % 2)
                last = None
                for j in range(CJ):
                    last = v.tensor_scalar(
                        Ct[:, base + j * 128:base + j * 128 + 128],
                        Ct[:, base + j * 128:base + j * 128 + 128],
                        par[tm_names[m]][:, j:j + 1], None, ALU.mult)
                return last
            ev_scale[m] = S.step("v", fs, waits=[ev_biasg[m]], inc=sv)
        ev_scale_k[oc] = (ev_bev, ev_scale)

    def prep_m(oc, m):
        nb = (oc * 3 + m) % 2
        evw = S.step("g", ld(Ct[:, natW_col[nb]:natW_col[nb] + 2048],
                             wm_d[m][oc * 128:(oc + 1) * 128, :]),
                     waits=[natW_free[nb]], inc=g_nw[nb], n=16)
        dst = ap3(Ct, CW, slab_col(m, oc % 2), [[128, 16], [1, 128]])
        ev = S.step("s", lambda s_, nb=nb, dst=dst: s_.dma_start(
            out=dst, in_=Ct[:, natW_col[nb]:natW_col[nb] + 2048], transpose=True),
            waits=[evw, slab_free[m][oc % 2]], inc=s_sl[m * 2 + oc % 2], n=16)
        natW_free[nb] = ev
        ev_tr_k.setdefault(oc, [None] * 3)[m] = ev

    def stage_prep(oc):
        for m in range(3):
            prep_m(oc, m)

    def stage_main(oc):
        nonlocal rwkv_last, last_attn_group
        ev_bev, ev_scale = ev_scale_k[oc]
        groups = {}

        def emit_group(mi, nt, bank, bfree):
            def fg(t, mi=mi, oc=oc, nt=nt, bank=bank):
                base = slab_col(mi, oc % 2)
                p = nt % 2
                for j in range(CJ):
                    last = t.matmul(bank[p][:, :],
                                    Ct[:, base + j * 128:base + j * 128 + 128],
                                    Asl(j, nt * 512, 512),
                                    start=(j == 0), stop=(j == CJ - 1))
                return last
            waits = [ev_scale[mi]] + ev_xtr_all
            if bfree is not None:
                waits.append(bfree)
            return S.step("t", fg, waits=waits, inc=st)

        ev_sigs = [None] * NT

        def emit_sig(nt):
            ev_sigs[nt] = S.step("c", lambda c_, nt=nt, oc=oc: c_.activation(
                E[0][:, nt * 512:(nt + 1) * 512], pR[nt % 2][:, :], AF.Sigmoid,
                bias=biasSB[0][:, oc:oc + 1]),
                waits=[groups[("r", nt)], ev_bev, sig_free[nt]], inc=sc)
            bankR_free[nt % 2] = ev_sigs[nt]
        for nt in (0, 1):
            groups[("r", nt)] = emit_group(0, nt, pR, bankR_free[nt % 2])
        for nt in (0, 1):
            emit_sig(nt)
        for nt in (2, 3):
            groups[("r", nt)] = emit_group(0, nt, pR, bankR_free[nt % 2])
        for nt in (2, 3):
            emit_sig(nt)
        slab_free[0][oc % 2] = groups[("r", NT - 1)]
        if oc + 1 < 16:
            stage_bias(oc + 1)
        prep_m(oc + 2, 0) if oc + 2 < 16 else None
        ev_rws = [None] * NT

        def emit_rwkv(nt):
            c0 = 1 if nt == 0 else 0

            def fr(v, oc=oc, nt=nt, c0=c0):
                p = nt % 2
                return v.scalar_tensor_tensor(
                    B[:, oc * 2048 + nt * 512 + c0:oc * 2048 + (nt + 1) * 512],
                    pV[p][:, c0:512], biasSB[1][:, oc:oc + 1],
                    E[0][:, nt * 512 + c0:(nt + 1) * 512],
                    ALU.add, ALU.mult)
            ev_rws[nt] = S.step("v", fr, waits=[groups[("v", nt)], ev_sigs[nt]],
                                inc=sv)
            bankV_free[nt % 2] = ev_rws[nt]
            sig_free[nt] = ev_rws[nt]
            if nt == 0:
                def f0v(v, oc=oc):
                    return v.tensor_copy(v0t[:, oc:oc + 1], pV[0][:, 0:1])
                ev_f0v = S.step("v", f0v, waits=[groups[("v", 0)]], inc=sv)
                bankV_free[0] = ev_f0v
        for nt in (0, 1):
            groups[("v", nt)] = emit_group(1, nt, pV, bankV_free[nt % 2])
        for nt in (0, 1):
            emit_rwkv(nt)
        for nt in (2, 3):
            groups[("v", nt)] = emit_group(1, nt, pV, bankV_free[nt % 2])
        for nt in (2, 3):
            emit_rwkv(nt)
        slab_free[1][oc % 2] = groups[("v", NT - 1)]
        rwkv_last = ev_rws[NT - 1]
        prep_m(oc + 2, 1) if oc + 2 < 16 else None
        ev_c0 = None
        for nt in range(NT):
            if nt == 2:
                def f0k(v, oc=oc):
                    v.tensor_copy(k0t[:, oc:oc + 1], pK[0][:, 0:1])
                    return v.tensor_copy(s0t[:, oc:oc + 1], E[0][:, 0:1])
                ev_c0 = S.step("v", f0k, waits=[groups[("k", 0)], ev_sigs[0]],
                               inc=sv)
                bankK_free[0] = ev_c0
            bfree = bankK_free[0] if nt == 2 else None
            groups[("k", nt)] = emit_group(2, nt, pK, bfree)
        slab_free[2][oc % 2] = groups[("k", NT - 1)]
        last_attn_group = groups[("k", NT - 1)]
        sig_free[0] = ev_c0
        rwkv_last = ev_c0
        prep_m(oc + 2, 2) if oc + 2 < 16 else None
        if oc + 1 < 16:
            stage_scale(oc + 1)

    last_attn_group = None
    stage_prep(0)
    stage_bias(0)
    stage_scale(0)
    stage_prep(1)
    for oc in range(16):
        stage_main(oc)
        for _ in range(8):
            emit_prep_unit()

    def _v(fn, waits=(), fence=False):
        w = list(waits)
        if fence:
            w.append((sv, sv.n))
        return S.step("v", fn, waits=w, inc=sv)

    _v(lambda v: v.scalar_tensor_tensor(wwt[:, :], k0t[:, :], 0.0,
                                        biasSB[2][:, :], ALU.add, ALU.add))
    _v(lambda v: v.tensor_add(wwt[:, :], wwt[:, :], par["tf"][:, :]), fence=True)
    _v(lambda v: v.tensor_max(qqt[:, :], par["pp"][:, :], wwt[:, :]), fence=True)
    _v(lambda v: v.tensor_sub(t1t[:, :], par["pp"][:, :], qqt[:, :]), fence=True)
    ev = _v(lambda v: v.tensor_sub(t2t[:, :], wwt[:, :], qqt[:, :]), fence=True)

    def fexp(c_):
        c_.activation(e1t[:, :], t1t[:, :], AF.Exp)
        return c_.activation(e2t[:, :], t2t[:, :], AF.Exp)
    ev_exp = S.step("c", fexp, waits=[ev], inc=sc)
    _v(lambda v: v.tensor_add(v0t[:, :], v0t[:, :], biasSB[1][:, :]), waits=[ev_exp])
    _v(lambda v: v.tensor_mul(numt[:, :], e1t[:, :], par["aa"][:, :]))
    _v(lambda v: v.tensor_mul(t1t[:, :], e2t[:, :], v0t[:, :]), fence=True)
    _v(lambda v: v.tensor_add(numt[:, :], numt[:, :], t1t[:, :]), fence=True)
    _v(lambda v: v.scalar_tensor_tensor(dent[:, :], e1t[:, :], 0.0,
                                        par["bb"][:, :], ALU.add, ALU.mult))
    _v(lambda v: v.tensor_add(dent[:, :], dent[:, :], e2t[:, :]), fence=True)
    _v(lambda v: v.reciprocal(t2t[:, :], dent[:, :]), fence=True)
    _v(lambda v: v.tensor_mul(wk0t[:, :], numt[:, :], t2t[:, :]), fence=True)
    _v(lambda v: v.tensor_mul(wk0t[:, :], wk0t[:, :], s0t[:, :]), fence=True)

    def fscat(v):
        last = None
        for oc in range(16):
            last = v.tensor_copy(B[:, oc * 2048:oc * 2048 + 1], wk0t[:, oc:oc + 1])
        return last
    col0_scatter = _v(fscat, fence=True)

    woq_free = [None, None]
    ystg_free = [None, None]
    ystore = {}
    natWo_free = None
    last_out_group = None
    woq_free[0] = last_attn_group
    woq_free[1] = last_attn_group
    out_evac_last = [None, None]
    for oq in range(4):
        qp = oq % 2
        ev_qt2 = [None, None]
        for tt in range(4):
            nb = tt % 2
            evw = S.step("g", ld(dqo[nb][:, :],
                                 wo_d[(oq * 4 + tt) * 128:(oq * 4 + tt + 1) * 128, :]),
                         waits=[dq_free[nb]], inc=g_dq[nb], n=16)
            dst = ap3(Ct, CW, woq_col[qp] + tt * 128, [[512, 16], [1, 128]])
            ev_qt2[nb] = S.step("s", lambda s_, nb=nb, dst=dst: s_.dma_start(
                out=dst, in_=dqo[nb][:, :], transpose=True),
                waits=[evw, woq_free[qp] if tt == 0 else None],
                inc=s_qt if nb == 0 else s_qtb, n=16)
            dq_free[nb] = ev_qt2[nb]
        ev_qt = ev_qt2
        order = list(range(1, 16)) + [0]
        for gi, ncl in enumerate(order):
            p = gi % 2

            def fo_(t, ncl=ncl, qp=qp, p=p):
                for j in range(CJ):
                    last = t.matmul(P[p][:, :],
                                    B[:, j * 2048 + ncl * 128:j * 2048 + ncl * 128 + 128],
                                    Ct[:, woq_col[qp] + j * 512:woq_col[qp] + (j + 1) * 512],
                                    start=(j == 0), stop=(j == CJ - 1))
                return last
            ev_g = S.step("t", fo_,
                          waits=ev_qt + [rwkv_last if ncl != 0 else col0_scatter,
                                         ystg_free[p]], inc=st)
            last_out_group = ev_g
            ev_e = S.step("v", lambda v, p=p: v.tensor_copy(
                E[1][:, p * 512:(p + 1) * 512], P[p][:, :]), waits=[ev_g], inc=sv)
            out_evac_last[p] = ev_e
            ev_s = S.step("s", lambda s_, p=p, ncl=ncl, oq=oq: s_.dma_start(
                out=yscr_d[ncl * 128:(ncl + 1) * 128, oq * 512:(oq + 1) * 512],
                in_=E[1][:, p * 512:(p + 1) * 512]), waits=[ev_e], inc=s_y[p], n=16)
            ystg_free[p] = ev_s
            ystore[(oq, ncl)] = ev_s
        woq_free[qp] = last_out_group
        for _ in range(8):
            emit_prep_unit()
    while _pi[0] < len(prep_units):
        emit_prep_unit()
    last_ystore = tot(s_y[0], s_y[1])
    ev_prep_done = tot(s_stb[0], s_stb[1])

    S.step("g", ld(Ct[0:16, bupB_col:bupB_col + 8192], upB_d),
           waits=[last_out_group], inc=g_lo, n=16)
    S.step("g", ld(Ct[0:16, bdnB_col:bdnB_col + 2048], dnB_d), inc=g_lo, n=16)
    S.step("g", ld(Ct[:, adn_col:adn_col + 1024],
                   bass.AP(dnA_d.tensor, 0, [[16, 128], [2048, 64], [1, 16]]), slow=False),
           inc=g_lo, n=16)
    S.step("g", ld(Ct[:, aup_col:aup_col + 256],
                   bass.AP(upA_d.tensor, 0, [[16, 128], [2048, 16], [1, 16]])),
           inc=g_lo, n=16)
    ev_lora = S.step("g", ld(Ct[0:1, bdnv_col:bdnv_col + 2048],
                             bass.AP(bdn_d.tensor, 0, [[2048, 1], [1, 2048]])),
                     inc=g_lo, n=16)
    ev_ones = S.step("v", lambda v: v.memset(Ct[0:1, ones_col:ones_col + 128], 1.0),
                     waits=[last_out_group], inc=sv)

    def ln_chain(tile_t, slot, dump, waits_data):
        t = lnT[slot]
        sy_, sq_, mn_, vr_, nv_, sd_, rs_ = [x[:, :] for x in t]
        s4 = lnS4[slot]
        tile = tile_t[:, :]
        ev1 = S.step("v", lambda v: v.reduce_sum(sy_, tile, AX.X),
                     waits=list(waits_data), inc=sv)

        def fsq(v):
            last = None
            for i in range(4):
                last = v.scalar_tensor_tensor(
                    dump[:, :], tile_t[:, i * 512:(i + 1) * 512], 0.0,
                    tile_t[:, i * 512:(i + 1) * 512], ALU.add, ALU.mult,
                    accum_out=s4[:, i:i + 1])
            return last
        ev2 = S.step("v", fsq, waits=list(waits_data), inc=sv)
        S.step("v", lambda v: v.reduce_sum(sq_, s4[:, :], AX.X), waits=[ev2], inc=sv)
        S.step("v", lambda v: v.tensor_scalar(mn_, sy_, INV_C, None, ALU.mult),
               waits=[(sv, sv.n)], inc=sv)
        S.step("v", lambda v: v.tensor_scalar(vr_, sq_, INV_C, None, ALU.mult),
               waits=[(sv, sv.n)], inc=sv)
        ev5 = S.step("v", lambda v: v.scalar_tensor_tensor(nv_, mn_, mn_, vr_,
                                                           ALU.mult, ALU.subtract),
                     waits=[(sv, sv.n)], inc=sv)
        ev6 = S.step("c", lambda c_: c_.activation(sd_, nv_, AF.Sqrt,
                                                   bias=epst[:, :], scale=-1.0),
                     waits=[ev5], inc=sc)
        S.step("v", lambda v: v.reciprocal(rs_, sd_), waits=[ev6], inc=sv)
        ev8 = S.step("v", lambda v: v.tensor_scalar(tile, tile, mn_, rs_,
                                                    ALU.subtract, ALU.mult),
                     waits=[(sv, sv.n)], inc=sv)
        return ev8, ev2

    yln_free = [[], []]
    E_free = [None, None]
    hbf_free = [None, None]
    early_ul = {}
    hstore = [None] * NCK
    hT_done = [None] * NCK
    prep_all_done = [(sv, sv.n), (sc, sc.n)]
    last_sq = None
    for ncl in range(NCK):
        b = ncl % 2
        ev_y = S.step("g", ld(yln[b][:, :], yscr_d[ncl * 128:(ncl + 1) * 128, :]),
                      waits=last_ystore + prep_all_done + yln_free[b],
                      inc=g_yl[b], n=16)
        ev_x = S.step("g", ld(E[b][:, :], x_d[ncl * 128:(ncl + 1) * 128, :]),
                      waits=[E_free[b]], inc=g_E[b], n=16)
        ev_add = S.step("v", lambda v, b=b: v.tensor_add(yln[b][:, :], yln[b][:, :],
                                                         E[b][:, :]),
                        waits=[ev_y, ev_x], inc=sv)
        E_free[b] = ev_add
        ev_ap, ev_sq = ln_chain(yln[b], b, P[4], [ev_add, last_sq])
        last_sq = ev_sq
        hstore[ncl] = S.step("s", lambda s_, b=b, ncl=ncl: s_.dma_start(
            out=hscr_d[ncl * 128:(ncl + 1) * 128, :], in_=yln[b][:, :]),
            waits=[ev_ap], inc=s_hst[b], n=16)
        ev_cv = S.step("v", lambda v, b=b: v.tensor_copy(
            Ct[:, b * 2048:(b + 1) * 2048], yln[b][:, :]),
            waits=[ev_ap, hbf_free[b]], inc=sv)
        hT_done[ncl] = S.step("s", lambda s_, b=b, ncl=ncl: s_.dma_start(
            out=A[:, :, ncl * 128:(ncl + 1) * 128],
            in_=Ct[:, b * 2048:(b + 1) * 2048], transpose=True),
            waits=[ev_cv], inc=s_hT[b], n=16)
        hbf_free[b] = hT_done[ncl]
        yln_free[b] = [hstore[ncl], ev_cv]
        if ncl == 2:
            for hcq in (0, 1):
                early_ul[hcq] = S.step(
                    "g", ld(dqo[hcq][:, :], upscr_d[hcq * 128:(hcq + 1) * 128, :]),
                    waits=list(ev_prep_done) + [dq_free[hcq]], inc=g_dq[hcq], n=16)
    ev_hT_all = tot(s_hT[0], s_hT[1])
    ev_hst_all = tot(s_hst[0], s_hst[1])

    slabL_free = [None, None]
    relu_done = [[None] * HCN for _ in range(4)]
    x2_reader = [None] * HCN
    pU = [P[0], P[1]]
    down_evac = {}
    down_sq = {}
    wdt_free = [None, None]
    bank_guard = {i: None for i in range(8)}
    bank_guard[4] = last_sq

    for s_ in range(4):
        def ftu(t, s_=s_):
            last = None
            for j in range(CJ):
                last = t.matmul(P[3][0:16, :],
                                Ct[:, aup_col + j * 16:aup_col + (j + 1) * 16],
                                Asl(j, s_ * 512, 512),
                                start=(j == 0), stop=(j == CJ - 1))
            return last
        wtu = [hT_done[4 * s_ + 2], hT_done[4 * s_ + 3], ev_lora]
        if s_ > 0:
            wtu.append(down_evac[(s_ - 1, 1, 0, 3)])
        ev_tu = S.step("t", ftu, waits=wtu, inc=st)
        ev_tue = S.step("v", lambda v: v.tensor_copy(Ct[0:16, tup_col:tup_col + 512],
                                                     P[3][0:16, :]),
                        waits=[ev_tu], inc=sv)
        ev_tdn_last = None
        for hc in range(HCN):
            b = hc % 2
            if s_ == 0 and hc < 2:
                ev_l = early_ul[hc]
            else:
                wl = list(ev_prep_done) + [slabL_free[b]]
                ev_l = S.step("g", ld(dqo[b][:, :], upscr_d[hc * 128:(hc + 1) * 128, :]),
                              waits=wl, inc=g_dq[b], n=16)

            def fu(t, b=b, hc=hc, s_=s_):
                for j in range(CJ):
                    t.matmul(pU[hc % 2][:, :], dqo[b][:, j * 128:(j + 1) * 128],
                             Asl(j, s_ * 512, 512), start=(j == 0), stop=False)
                return t.matmul(pU[hc % 2][:, :],
                                Ct[0:16, bupB_col + hc * 128:bupB_col + hc * 128 + 128],
                                Ct[0:16, tup_col:tup_col + 512],
                                start=False, stop=True)
            wu = [ev_l, ev_tue, hT_done[4 * s_ + 2], hT_done[4 * s_ + 3]]
            if hc >= 2:
                wu.append(relu_done[s_][hc - 2])
            if s_ > 0:
                wu.append(x2_reader[hc])
                if hc < 2:
                    wu.append(down_evac[(s_ - 1, 1, 0, hc % 2)])
            elif hc < 2:
                wu.append(out_evac_last[hc % 2])
            if bank_guard.get(hc % 2) is not None:
                wu.append(bank_guard[hc % 2])
            ev_u = S.step("t", fu, waits=wu, inc=st)
            slabL_free[b] = ev_u
            relu_done[s_][hc] = S.step("c", lambda c_, hc=hc: c_.activation(
                B[:, hc * 512:(hc + 1) * 512], pU[hc % 2][:, :], AF.Relu,
                bias=bup_t[:, hc:hc + 1]), waits=[ev_u], inc=sc)

            def emit_tdn(hcx):
                def ftd(t, hcx=hcx):
                    return t.matmul(P[2][0:16, :],
                                    Ct[:, adn_col + hcx * 16:adn_col + (hcx + 1) * 16],
                                    B[:, hcx * 512:(hcx + 1) * 512],
                                    start=(hcx == 0), stop=(hcx == HCN - 1))
                wtd = [relu_done[s_][hcx]]
                if hcx == 0:
                    if s_ > 0:
                        wtd.append(down_evac[(s_ - 1, 1, 0, 2)])
                    if bank_guard.get(2) is not None:
                        wtd.append(bank_guard[2])
                return S.step("t", ftd, waits=wtd, inc=st)
            if hc >= 1:
                emit_tdn(hc - 1)
            if hc == HCN - 1:
                ev_tdn_last = emit_tdn(hc)
        ev_tde = S.step("v", lambda v: v.tensor_copy(Ct[0:16, tdn_col:tdn_col + 512],
                                                     P[2][0:16, :]),
                        waits=[ev_tdn_last], inc=sv)
        for pair in range(2):
            ev_h = [None, None]
            for l in range(2):
                ncl = s_ * 4 + pair * 2 + l
                wE = list(ev_hst_all) + [E_free[l]]
                if pair == 1:
                    wE.append(down_evac[(s_, 0, l, 3)])
                elif s_ > 0:
                    wE.append(down_evac[(s_ - 1, 1, l, 3)])
                ev_h[l] = S.step("g", ld(E[l][:, :], hscr_d[ncl * 128:(ncl + 1) * 128, :]),
                                 waits=wE, inc=g_E[l], n=16)
            for hc in range(HCN):
                wb = hc % 2
                wwd = list(ev_prep_done) + [wdt_free[wb], last_out_group]
                if s_ == 0 and pair == 0 and hc < 2:
                    wwd += ev_hT_all
                ev_w = S.step("c" if wb == 0 else "s",
                              lambda e_, wb=wb, hc=hc: e_.dma_start(
                    out=Ct[:, wdt_col[wb]:wdt_col[wb] + 2048],
                    in_=dnscr_d[hc * 128:(hc + 1) * 128, :]),
                    waits=wwd, inc=s_wd[wb], n=16)

                def fd(t, hc=hc, pair=pair, wb=wb):
                    last = None
                    for l in range(2):
                        for dt in range(4):
                            last = t.matmul(
                                P[l * 4 + dt][:, :],
                                B[:, hc * 512 + (pair * 2 + l) * 128:
                                  hc * 512 + (pair * 2 + l) * 128 + 128],
                                Ct[:, wdt_col[wb] + dt * 512:wdt_col[wb] + (dt + 1) * 512],
                                start=(hc == 0), stop=False)
                    return last
                wd = [ev_w, relu_done[s_][hc]]
                if hc == 0:
                    if pair == 0:
                        wd += [relu_done[s_][HCN - 1], ev_tde]
                        for bk in range(8):
                            if bank_guard.get(bk) is not None:
                                wd.append(bank_guard[bk])
                    else:
                        wd += [down_evac[(s_, 0, l2, dt2)]
                               for l2 in range(2) for dt2 in range(4)]
                        wd += [down_sq.get((s_, 0, 0)), down_sq.get((s_, 0, 1))]
                ev_d = S.step("t", fd, waits=wd, inc=st)
                wdt_free[wb] = ev_d
                if pair == 1:
                    x2_reader[hc] = ev_d
            grp_done = {}
            for l in range(2):
                for dt in range(4):
                    def ff(t, l=l, dt=dt, pair=pair):
                        t.matmul(P[l * 4 + dt][:, :],
                                 Ct[0:16, tdn_col + (pair * 2 + l) * 128:
                                    tdn_col + (pair * 2 + l) * 128 + 128],
                                 Ct[0:16, bdnB_col + dt * 512:bdnB_col + (dt + 1) * 512],
                                 start=False, stop=False)
                        return t.matmul(P[l * 4 + dt][:, :],
                                        Ct[0:1, ones_col:ones_col + 128],
                                        Ct[0:1, bdnv_col + dt * 512:bdnv_col + (dt + 1) * 512],
                                        start=False, stop=True)
                    grp_done[(l, dt)] = S.step("t", ff, waits=[ev_ones, ev_lora], inc=st)
            for l in range(2):
                ncl = s_ * 4 + pair * 2 + l
                for dt in range(4):
                    def fe(v, l=l, dt=dt):
                        return v.tensor_add(yln[l][:, dt * 512:(dt + 1) * 512],
                                            P[l * 4 + dt][:, :],
                                            E[l][:, dt * 512:(dt + 1) * 512])
                    down_evac[(s_, pair, l, dt)] = S.step(
                        "v", fe, waits=[grp_done[(l, dt)], ev_h[l]] + yln_free[l],
                        inc=sv)
                ev_ap, ev_sq = ln_chain(yln[l], l, P[l * 4],
                                        [down_evac[(s_, pair, l, 3)]])
                down_sq[(s_, pair, l)] = ev_sq
                bank_guard[l * 4] = ev_sq
                ev_so = S.step("s", lambda s2, l=l, ncl=ncl: s2.dma_start(
                    out=out_d[ncl * 128:(ncl + 1) * 128, :], in_=yln[l][:, :]),
                    waits=[ev_ap], inc=s_out[l], n=16)
                yln_free[l] = [ev_so]
                E_free[l] = down_evac[(s_, pair, l, 3)]

    dma_sems = ([g_par, g_wo, g_lo] + g_dq + g_qi + g_sc + g_nw + g_E + g_yl
                + s_x + s_sl + s_ut + s_stb + [s_qt] + s_y + s_out + s_hst
                + s_hT + s_wd)
    S.step("s", None, waits=tot(*dma_sems))
    S.step("g", None, waits=tot(s_out[0], s_out[1]))

    S.check_deadlock()

    with es, nc.Block() as block:
        def runner(engname):
            def run(eng):
                for w, fn, inc, n in S.streams[engname]:
                    for sem, val in w:
                        eng.wait_ge(sem.h, val)
                    if fn is None:
                        continue
                    last = fn(eng)
                    if inc is not None:
                        last.then_inc(inc.h, n)
            return run
        block.gpsimd(runner("g"))
        block.sync(runner("s"))
        block.tensor(runner("t"))
        block.vector(runner("v"))
        block.scalar(runner("c"))
    return nc


def kernel(**inputs):
    x = np.asarray(inputs["L0_attn_x"])

    def vec(name):
        return np.ascontiguousarray(np.asarray(inputs[name]).reshape(-1))
    common = {
        "w_rec": np.ascontiguousarray(inputs["L0_attn_w_rec"]),
        "w_value": np.ascontiguousarray(inputs["L0_attn_w_value"]),
        "w_key": np.ascontiguousarray(inputs["L0_attn_w_key"]),
        "w_out": np.ascontiguousarray(inputs["L0_attn_w_out"]),
        "tf": vec("L0_attn_time_first"),
        "pp": vec("L0_attn_pp"),
        "aa": vec("L0_attn_aa"),
        "bb": vec("L0_attn_bb"),
        "tmr": vec("L0_attn_time_mix_r"),
        "tmv": vec("L0_attn_time_mix_v"),
        "tmk": vec("L0_attn_time_mix_k"),
        "xx": vec("L0_attn_xx"),
        "w_up_q": np.ascontiguousarray(inputs["L0_ffn_w_up_q"]),
        "w_up_scale": np.ascontiguousarray(inputs["L0_ffn_w_up_scale"]),
        "b_up": vec("L0_ffn_b_up"),
        "up_lora_a": np.ascontiguousarray(inputs["L0_ffn_w_up_lora_a"]),
        "up_lora_b": np.ascontiguousarray(inputs["L0_ffn_w_up_lora_b"]),
        "w_down_q": np.ascontiguousarray(inputs["L0_ffn_w_down_q"]),
        "w_down_scale": np.ascontiguousarray(inputs["L0_ffn_w_down_scale"]),
        "b_down": vec("L0_ffn_b_down"),
        "dn_lora_a": np.ascontiguousarray(inputs["L0_ffn_w_down_lora_a"]),
        "dn_lora_b": np.ascontiguousarray(inputs["L0_ffn_w_down_lora_b"]),
    }
    nc = build()
    in_maps = [dict(common, x=np.ascontiguousarray(x[i])) for i in range(8)]
    res = run_bass_kernel_spmd(nc, in_maps, core_ids=list(range(8)))
    return np.stack([np.asarray(r["out"], dtype=np.float32) for r in res.results],
                    axis=0)


# version 12
# speedup vs baseline: 1.0232x; 1.0232x over previous
import numpy as np
from contextlib import ExitStack
import concourse.bass as bass
import concourse.mybir as mybir
from concourse.bass_utils import run_bass_kernel_spmd

F32 = mybir.dt.float32
BF16 = mybir.dt.bfloat16
I32 = mybir.dt.int32
AF = mybir.ActivationFunctionType
ALU = mybir.AluOpType
AX = mybir.AxisListType

TOK = 2048
C = 2048
H = 8192
CJ = 16
NT = 4
NCK = 16
HCN = 64
INV_C = 1.0 / 2048.0
FULL_K = True

ENGS = ("g", "s", "t", "v", "c")


class _Sem:
    __slots__ = ("name", "n", "h")

    def __init__(self, name):
        self.name, self.n, self.h = name, 0, None


class Sched:
    def __init__(self):
        self.streams = {e: [] for e in ENGS}
        self.sems = []
        self.wm = {e: {} for e in ENGS}

    def sem(self, name):
        s = _Sem(name)
        self.sems.append(s)
        return s

    def step(self, eng, fn, waits=(), inc=None, n=1):
        w = []
        for item in waits:
            if item is None:
                continue
            sem, val = item
            if val is None or val <= 0:
                continue
            if self.wm[eng].get(sem.name, 0) >= val:
                continue
            self.wm[eng][sem.name] = val
            w.append((sem, val))
        ev = None
        if inc is not None:
            inc.n += n
            ev = (inc, inc.n)
        self.streams[eng].append((w, fn, inc, n))
        return ev

    def check_deadlock(self):
        ptr = {e: 0 for e in ENGS}
        val = {s.name: 0 for s in self.sems}
        progress = True
        while progress:
            progress = False
            for e in ENGS:
                while ptr[e] < len(self.streams[e]):
                    w, fn, inc, n = self.streams[e][ptr[e]]
                    if any(val[s.name] < v for s, v in w):
                        break
                    if inc is not None:
                        val[inc.name] += n
                    ptr[e] += 1
                    progress = True
        stuck = {e: (ptr[e], len(self.streams[e])) for e in ENGS
                 if ptr[e] < len(self.streams[e])}
        if stuck:
            det = []
            for e, (p, tot) in stuck.items():
                w = self.streams[e][p][0]
                bad = [(s.name, v, val[s.name]) for s, v in w if val[s.name] < v]
                det.append(f"{e}: step {p}/{tot} waits {bad}")
            raise RuntimeError("schedule deadlock: " + "; ".join(det))


def build():
    nc = bass.Bass("TRN2", target_bir_lowering=False)

    x_d = nc.dram_tensor("x", [TOK, C], F32, kind="ExternalInput").ap()
    wm_d = [nc.dram_tensor(nm, [C, C], F32, kind="ExternalInput").ap()
            for nm in ("w_rec", "w_value", "w_key")]
    wo_d = nc.dram_tensor("w_out", [C, C], F32, kind="ExternalInput").ap()
    vecs = {nm: nc.dram_tensor(nm, [C], F32, kind="ExternalInput").ap()
            for nm in ("tf", "pp", "aa", "bb", "tmr", "tmv", "tmk", "xx")}
    upq_d = nc.dram_tensor("w_up_q", [H, C], I32, kind="ExternalInput").ap()
    upsc_d = nc.dram_tensor("w_up_scale", [H, 32], F32, kind="ExternalInput").ap()
    bup_d = nc.dram_tensor("b_up", [H], F32, kind="ExternalInput").ap()
    upA_d = nc.dram_tensor("up_lora_a", [C, 16], F32, kind="ExternalInput").ap()
    upB_d = nc.dram_tensor("up_lora_b", [16, H], F32, kind="ExternalInput").ap()
    dnq_d = nc.dram_tensor("w_down_q", [C, H], I32, kind="ExternalInput").ap()
    dnsc_d = nc.dram_tensor("w_down_scale", [C, 128], F32, kind="ExternalInput").ap()
    bdn_d = nc.dram_tensor("b_down", [C], F32, kind="ExternalInput").ap()
    dnA_d = nc.dram_tensor("dn_lora_a", [H, 16], F32, kind="ExternalInput").ap()
    dnB_d = nc.dram_tensor("dn_lora_b", [16, C], F32, kind="ExternalInput").ap()
    out_d = nc.dram_tensor("out", [TOK, C], F32, kind="ExternalOutput").ap()
    upscr_d = nc.dram_tensor("up_scr", [H, 2048], BF16).ap()
    dnscr_d = nc.dram_tensor("dn_scr", [C, H], BF16).ap()
    yscr_d = nc.dram_tensor("y_scr", [TOK, C], F32).ap()
    hscr_d = nc.dram_tensor("h_scr", [TOK, C], F32).ap()

    es = ExitStack()

    SBB = 16512

    def sb(name, shape, dtype, off):
        return nc.alloc_sbuf_tensor_at(name, shape, dtype, offset=SBB + off)

    A = sb("A", [128, 16, 2048], BF16, 0)
    B = sb("B", [128, 32768], BF16, 65536)
    CW = 18816
    Ct = sb("Cx", [128, CW], BF16, 131072)
    DBASE = 131072 + 37632
    qin = [sb(f"qin{i}", [128, 2048], I32, DBASE + i * 8192) for i in range(2)]
    dqo = [sb(f"dqo{i}", [128, 2048], BF16, DBASE + 16384 + i * 4096) for i in range(2)]
    yln = [sb(f"yln{i}", [128, 2048], F32, DBASE + i * 8192) for i in range(2)]
    EBASE = DBASE + 24576
    E = [sb(f"Ee{i}", [128, 2048], F32, EBASE + i * 8192) for i in range(2)]
    FBASE = EBASE + 16384
    fo = [FBASE]

    def fsb(name, shape, dtype):
        t = sb(name, shape, dtype, fo[0])
        width = 1
        for d in shape[1:]:
            width *= d
        fo[0] += (width * mybir.dt.size(dtype) + 31) // 32 * 32
        assert SBB + fo[0] <= 229344, f"F overflow at {name}: {fo[0]}"
        return t

    par = {nm: fsb("p_" + nm, [128, 16], F32) for nm in
           ("tf", "pp", "aa", "bb", "tmr", "tmv", "tmk", "xx")}
    cmf = [fsb(f"cmf{m}", [128, 16], F32) for m in range(3)]
    cmb = [fsb(f"cmb{m}", [128, 16], BF16) for m in range(3)]
    biasSB = [fsb(f"biasSB{m}", [128, 16], F32) for m in range(3)]
    bup_t = fsb("bup", [128, 64], F32)
    upsct = [fsb(f"upsct{i}", [128, 32], F32) for i in range(2)]
    dnsct = [fsb(f"dnsct{i}", [128, 32], F32) for i in range(2)]
    epst = fsb("epst", [128, 1], F32)
    k0t = fsb("k0t", [128, 16], F32)
    v0t = fsb("v0t", [128, 16], F32)
    s0t = fsb("s0t", [128, 16], F32)
    wwt = fsb("wwt", [128, 16], F32)
    qqt = fsb("qqt", [128, 16], F32)
    t1t = fsb("t1t", [128, 16], F32)
    t2t = fsb("t2t", [128, 16], F32)
    e1t = fsb("e1t", [128, 16], F32)
    e2t = fsb("e2t", [128, 16], F32)
    numt = fsb("numt", [128, 16], F32)
    dent = fsb("dent", [128, 16], F32)
    wk0t = fsb("wk0t", [128, 16], F32)
    lnT = [[fsb(f"ln{n}{i}", [128, 1], F32) for n in
            ("sy", "sq", "mn", "vr", "nv", "sd", "rs")] for i in range(2)]
    lnS4 = [fsb(f"lns4{i}", [128, 4], F32) for i in range(2)]

    slab_col = lambda m, p: (m * 2 + p) * 2048
    natW_col = [12288, 14336]
    upstg_col = 16384
    natWo_col = 16384
    woq_col = [0, 8192]
    wdt_col = [0, 2048]
    bupB_col = 4096
    bdnB_col = 12288
    adn_col = 14336
    aup_col = 15360
    bdnv_col = 15616
    ones_col = 17664
    tup_col = 17792
    tdn_col = 18304

    P = [es.enter_context(nc.psum_tensor(f"pb{i}", [128, 512], F32)) for i in range(8)]

    S = Sched()
    st = S.sem("st")
    sv = S.sem("sv")
    sc = S.sem("sc")
    g_par = S.sem("g_par")
    g_wo = S.sem("g_wo")
    g_lo = S.sem("g_lo")
    g_dq = [S.sem(f"g_dq{i}") for i in range(2)]
    g_qi = [S.sem(f"g_qi{i}") for i in range(2)]
    g_sc = [S.sem(f"g_sc{i}") for i in range(2)]
    g_nw = [S.sem(f"g_nw{i}") for i in range(2)]
    g_E = [S.sem(f"g_E{i}") for i in range(2)]
    g_yl = [S.sem(f"g_yl{i}") for i in range(2)]
    s_x = [S.sem(f"s_x{i}") for i in range(2)]
    s_sl = [S.sem(f"s_sl{i}") for i in range(6)]
    s_ut = [S.sem(f"s_ut{i}") for i in range(2)]
    s_stb = [S.sem(f"s_stb{i}") for i in range(2)]
    s_qt = S.sem("s_qt")
    s_qtb = S.sem("s_qtb")
    s_y = [S.sem(f"s_y{i}") for i in range(2)]
    s_out = [S.sem(f"s_out{i}") for i in range(2)]
    s_hst = [S.sem(f"s_hst{i}") for i in range(2)]
    s_hT = [S.sem(f"s_hT{i}") for i in range(2)]
    s_wd = [S.sem(f"s_wd{i}") for i in range(2)]
    for sem in S.sems:
        sem.h = es.enter_context(nc.semaphore(sem.name))

    def tot(*sems):
        return [(s, s.n) for s in sems]

    def ap2(tens, width, off_el, w):
        return bass.AP(tens, off_el, [[width, 128]] + [[1, w]])

    def ap3(tens, width, off_el, dims):
        return bass.AP(tens, off_el, [[width, 128]] + dims)

    def Asl(j, a, w):
        return bass.AP(A, j * 2048 + a, [[32768, 128], [1, w]])

    def ld(dst_ap, src_ap, slow=False):
        def fn(g, d=dst_ap, s_=src_ap, sl=slow):
            if sl:
                return g.dma_start(out=d, in_=s_, allow_slow_non_contiguous=True)
            return g.dma_start(out=d, in_=s_)
        return fn

    ev_par = None
    for nm, t in par.items():
        src = bass.AP(vecs[nm].tensor, 0, [[1, 128], [128, 16]])
        ev_par = S.step("g", ld(t[:, :], src, slow=True), inc=g_par, n=16)
    ev_par = S.step("g", ld(bup_t[:, :],
                            bass.AP(bup_d.tensor, 0, [[1, 128], [128, 64]]), slow=True),
                    inc=g_par, n=16)
    S.step("v", lambda v: v.memset(epst[:, :], 1e-5), inc=sv)
    tm_names = ("tmr", "tmv", "tmk")
    ev_cmb = [None] * 3
    for m in range(3):
        tmt = par[tm_names[m]]
        ev = S.step("v", lambda v, m=m, tmt=tmt: v.tensor_scalar(
            cmf[m][:, :], tmt[:, :], -1.0, 1.0, ALU.mult, op1=ALU.add),
            waits=[ev_par], inc=sv)
        ev = S.step("v", lambda v, m=m: v.tensor_mul(
            cmf[m][:, :], cmf[m][:, :], par["xx"][:, :]),
            waits=[(sv, sv.n)], inc=sv)
        ev_cmb[m] = S.step("v", lambda v, m=m: v.tensor_copy(
            cmb[m][:, :], cmf[m][:, :]), waits=[(sv, sv.n)], inc=sv)

    ev_xtr = [None] * 16
    dq_free = [None, None]

    prep_units = [("u", hc) for hc in range(HCN)]
    for q in range(4):
        for dc in range(16):
            prep_units.append(("d", dc, q))
    qin_free = [None, None]
    upscr_done = [None] * HCN
    dn_qdone = [None] * 4
    last_upstg = [None]
    _pi = [0]

    def emit_prep_unit():
        i = _pi[0]
        if i >= len(prep_units):
            return
        _pi[0] += 1
        u = prep_units[i]
        b = i % 2
        if u[0] == "u":
            hc = u[1]
            evq = S.step("c", ld(qin[b][:, :], upq_d[hc * 128:(hc + 1) * 128, :]),
                         waits=[qin_free[b]], inc=g_qi[b], n=16)
            evs = S.step("c", ld(upsct[b][:, :], upsc_d[hc * 128:(hc + 1) * 128, :]),
                         inc=g_sc[b], n=16)
            scb = bass.AP(upsct[b], 0, [[32, 128], [1, 32], [0, 64]])
            evd = S.step("v", lambda v, b=b, scb=scb: v.scalar_tensor_tensor(
                dqo[b][:, :], qin[b][:, :], -7.5, scb, ALU.add, ALU.mult),
                waits=[evq, evs, dq_free[b]], inc=sv)
            qin_free[b] = evd
            dst = ap3(Ct, CW, upstg_col, [[128, 16], [1, 128]])
            evtp = S.step("s", lambda s_, b=b, dst=dst: s_.dma_start(
                out=dst, in_=dqo[b][:, :], transpose=True), waits=[evd],
                inc=s_ut[b], n=16)
            evst = S.step("s", lambda s_, hc=hc: s_.dma_start(
                out=upscr_d[hc * 128:(hc + 1) * 128, :],
                in_=Ct[:, upstg_col:upstg_col + 2048]), waits=[evtp],
                inc=s_stb[b], n=16)
            dq_free[b] = evst
            upscr_done[hc] = evst
            last_upstg[0] = evst
        else:
            dc, q = u[1], u[2]
            evq = S.step("c", ld(qin[b][:, :],
                                 dnq_d[dc * 128:(dc + 1) * 128, q * 2048:(q + 1) * 2048]),
                         waits=[qin_free[b]], inc=g_qi[b], n=16)
            evs = S.step("c", ld(dnsct[b][:, :],
                                 dnsc_d[dc * 128:(dc + 1) * 128, q * 32:(q + 1) * 32]),
                         inc=g_sc[b], n=16)
            scb = bass.AP(dnsct[b], 0, [[32, 128], [1, 32], [0, 64]])
            evd = S.step("v", lambda v, b=b, scb=scb: v.scalar_tensor_tensor(
                dqo[b][:, :], qin[b][:, :], -7.5, scb, ALU.add, ALU.mult),
                waits=[evq, evs, dq_free[b]], inc=sv)
            qin_free[b] = evd
            evst = S.step("s", lambda s_, dc=dc, q=q: s_.dma_start(
                out=dnscr_d[dc * 128:(dc + 1) * 128, q * 2048:(q + 1) * 2048],
                in_=dqo[b][:, :]), waits=[evd], inc=s_stb[b], n=16)
            dq_free[b] = evst
            dn_qdone[q] = evst

    natW_free = [None, None]
    slab_free = [[None, None] for _ in range(3)]
    bias_bank_free = [None, None]
    sig_free = [None, None, None, None]
    bankR_free = [None, None]
    bankV_free = [None, None]
    bankK_free = [None]
    rwkv_last = None
    pR, pV, pK, pBias = [P[0], P[1]], [P[2], P[3]], [P[4], P[5]], [P[6], P[7]]

    ev_tr_k = {}
    ev_bias_k = {}
    ev_scale_k = {}

    def stage_prep(oc):
        ev_tr = [None] * 3
        for m in range(3):
            nb = (oc * 3 + m) % 2
            evw = S.step("g", ld(Ct[:, natW_col[nb]:natW_col[nb] + 2048],
                                 wm_d[m][oc * 128:(oc + 1) * 128, :]),
                         waits=[natW_free[nb]], inc=g_nw[nb], n=16)
            dst = ap3(Ct, CW, slab_col(m, oc % 2), [[128, 16], [1, 128]])
            ev_tr[m] = S.step("s", lambda s_, nb=nb, dst=dst: s_.dma_start(
                out=dst, in_=Ct[:, natW_col[nb]:natW_col[nb] + 2048], transpose=True),
                waits=[evw, slab_free[m][oc % 2]], inc=s_sl[m * 2 + oc % 2], n=16)
            natW_free[nb] = ev_tr[m]
        ev_tr_k[oc] = ev_tr

    def stage_bias(oc):
        ev_tr = ev_tr_k[oc]
        ev_biasg = [None] * 3
        for m in range(3):
            def fb(t, m=m, oc=oc):
                base = slab_col(m, oc % 2)
                for j in range(CJ):
                    last = t.matmul(pBias[oc % 2][:, m:m + 1],
                                    Ct[:, base + j * 128:base + j * 128 + 128],
                                    cmb[m][:, j:j + 1],
                                    start=(j == 0), stop=(j == CJ - 1))
                return last
            ev_biasg[m] = S.step("t", fb,
                                 waits=[ev_tr[m], ev_cmb[m], bias_bank_free[oc % 2]],
                                 inc=st)
        ev_bias_k[oc] = ev_biasg

    def stage_scale(oc):
        ev_biasg = ev_bias_k[oc]

        def fbe(v, oc=oc):
            last = None
            for m in range(3):
                last = v.tensor_copy(biasSB[m][:, oc:oc + 1], pBias[oc % 2][:, m:m + 1])
            return last
        ev_bev = S.step("v", fbe, waits=[ev_biasg[2]], inc=sv)
        bias_bank_free[oc % 2] = ev_bev
        ev_scale = [None] * 3
        for m in range(3):
            def fs(v, m=m, oc=oc):
                base = slab_col(m, oc % 2)
                last = None
                for j in range(CJ):
                    last = v.tensor_scalar(
                        Ct[:, base + j * 128:base + j * 128 + 128],
                        Ct[:, base + j * 128:base + j * 128 + 128],
                        par[tm_names[m]][:, j:j + 1], None, ALU.mult)
                return last
            ev_scale[m] = S.step("v", fs, waits=[ev_biasg[m]], inc=sv)
        ev_scale_k[oc] = (ev_bev, ev_scale)

    def prep_m(oc, m):
        nb = (oc * 3 + m) % 2
        evw = S.step("g", ld(Ct[:, natW_col[nb]:natW_col[nb] + 2048],
                             wm_d[m][oc * 128:(oc + 1) * 128, :]),
                     waits=[natW_free[nb]], inc=g_nw[nb], n=16)
        dst = ap3(Ct, CW, slab_col(m, oc % 2), [[128, 16], [1, 128]])
        ev = S.step("s", lambda s_, nb=nb, dst=dst: s_.dma_start(
            out=dst, in_=Ct[:, natW_col[nb]:natW_col[nb] + 2048], transpose=True),
            waits=[evw, slab_free[m][oc % 2]], inc=s_sl[m * 2 + oc % 2], n=16)
        natW_free[nb] = ev
        ev_tr_k.setdefault(oc, [None] * 3)[m] = ev

    def stage_prep(oc):
        for m in range(3):
            prep_m(oc, m)

    def stage_main(oc):
        nonlocal rwkv_last, last_attn_group
        ev_bev, ev_scale = ev_scale_k[oc]
        groups = {}

        def emit_group(mi, nt, bank, bfree):
            def fg(t, mi=mi, oc=oc, nt=nt, bank=bank):
                base = slab_col(mi, oc % 2)
                p = nt % 2
                for j in range(CJ):
                    last = t.matmul(bank[p][:, :],
                                    Ct[:, base + j * 128:base + j * 128 + 128],
                                    Asl(j, nt * 512, 512),
                                    start=(j == 0), stop=(j == CJ - 1))
                return last
            waits = [ev_scale[mi]] + ev_xtr_all
            if bfree is not None:
                waits.append(bfree)
            return S.step("t", fg, waits=waits, inc=st)

        ev_sigs = [None] * NT

        def emit_sig(nt):
            ev_sigs[nt] = S.step("c", lambda c_, nt=nt, oc=oc: c_.activation(
                E[0][:, nt * 512:(nt + 1) * 512], pR[nt % 2][:, :], AF.Sigmoid,
                bias=biasSB[0][:, oc:oc + 1]),
                waits=[groups[("r", nt)], ev_bev, sig_free[nt]], inc=sc)
            bankR_free[nt % 2] = ev_sigs[nt]
        for nt in (0, 1):
            groups[("r", nt)] = emit_group(0, nt, pR, bankR_free[nt % 2])
        for nt in (0, 1):
            emit_sig(nt)
        for nt in (2, 3):
            groups[("r", nt)] = emit_group(0, nt, pR, bankR_free[nt % 2])
        for nt in (2, 3):
            emit_sig(nt)
        slab_free[0][oc % 2] = groups[("r", NT - 1)]
        if oc + 1 < 16:
            stage_bias(oc + 1)
        prep_m(oc + 2, 0) if oc + 2 < 16 else None
        ev_rws = [None] * NT

        def emit_rwkv(nt):
            c0 = 1 if nt == 0 else 0

            def fr(v, oc=oc, nt=nt, c0=c0):
                p = nt % 2
                return v.scalar_tensor_tensor(
                    B[:, oc * 2048 + nt * 512 + c0:oc * 2048 + (nt + 1) * 512],
                    pV[p][:, c0:512], biasSB[1][:, oc:oc + 1],
                    E[0][:, nt * 512 + c0:(nt + 1) * 512],
                    ALU.add, ALU.mult)
            ev_rws[nt] = S.step("v", fr, waits=[groups[("v", nt)], ev_sigs[nt]],
                                inc=sv)
            bankV_free[nt % 2] = ev_rws[nt]
            sig_free[nt] = ev_rws[nt]
            if nt == 0:
                def f0v(v, oc=oc):
                    return v.tensor_copy(v0t[:, oc:oc + 1], pV[0][:, 0:1])
                ev_f0v = S.step("v", f0v, waits=[groups[("v", 0)]], inc=sv)
                bankV_free[0] = ev_f0v
        for nt in (0, 1):
            groups[("v", nt)] = emit_group(1, nt, pV, bankV_free[nt % 2])
        for nt in (0, 1):
            emit_rwkv(nt)
        for nt in (2, 3):
            groups[("v", nt)] = emit_group(1, nt, pV, bankV_free[nt % 2])
        for nt in (2, 3):
            emit_rwkv(nt)
        slab_free[1][oc % 2] = groups[("v", NT - 1)]
        rwkv_last = ev_rws[NT - 1]
        prep_m(oc + 2, 1) if oc + 2 < 16 else None
        ev_c0 = None
        for nt in range(NT):
            if nt == 2:
                def f0k(v, oc=oc):
                    v.tensor_copy(k0t[:, oc:oc + 1], pK[0][:, 0:1])
                    return v.tensor_copy(s0t[:, oc:oc + 1], E[0][:, 0:1])
                ev_c0 = S.step("v", f0k, waits=[groups[("k", 0)], ev_sigs[0]],
                               inc=sv)
                bankK_free[0] = ev_c0
            bfree = bankK_free[0] if nt == 2 else None
            groups[("k", nt)] = emit_group(2, nt, pK, bfree)
        slab_free[2][oc % 2] = groups[("k", NT - 1)]
        last_attn_group = groups[("k", NT - 1)]
        sig_free[0] = ev_c0
        rwkv_last = ev_c0
        prep_m(oc + 2, 2) if oc + 2 < 16 else None
        if oc + 1 < 16:
            stage_scale(oc + 1)

    last_attn_group = None
    for tt in range(16):
        if tt == 4:
            stage_prep(0), stage_bias(0), stage_scale(0)
        elif tt == 8:
            stage_prep(1)
        ev_l = S.step("g", ld(dqo[tt % 2][:, :], x_d[tt * 128:(tt + 1) * 128, :]),
                      waits=[dq_free[tt % 2]], inc=g_dq[tt % 2], n=16)
        ev_xtr[tt] = S.step("s", lambda s_, tt=tt: s_.dma_start(
            out=A[:, :, tt * 128:(tt + 1) * 128], in_=dqo[tt % 2][:, :], transpose=True),
            waits=[ev_l], inc=s_x[tt % 2], n=16)
        dq_free[tt % 2] = ev_xtr[tt]
    ev_xtr_all = tot(s_x[0], s_x[1])

    for oc in range(16):
        stage_main(oc)
        for _ in range(7):
            emit_prep_unit()

    def _v(fn, waits=(), fence=False):
        w = list(waits)
        if fence:
            w.append((sv, sv.n))
        return S.step("v", fn, waits=w, inc=sv)

    _v(lambda v: v.scalar_tensor_tensor(wwt[:, :], k0t[:, :], 0.0,
                                        biasSB[2][:, :], ALU.add, ALU.add))
    _v(lambda v: v.tensor_add(wwt[:, :], wwt[:, :], par["tf"][:, :]), fence=True)
    _v(lambda v: v.tensor_max(qqt[:, :], par["pp"][:, :], wwt[:, :]), fence=True)
    _v(lambda v: v.tensor_sub(t1t[:, :], par["pp"][:, :], qqt[:, :]), fence=True)
    ev = _v(lambda v: v.tensor_sub(t2t[:, :], wwt[:, :], qqt[:, :]), fence=True)

    def fexp(c_):
        c_.activation(e1t[:, :], t1t[:, :], AF.Exp)
        return c_.activation(e2t[:, :], t2t[:, :], AF.Exp)
    ev_exp = S.step("c", fexp, waits=[ev], inc=sc)
    _v(lambda v: v.tensor_add(v0t[:, :], v0t[:, :], biasSB[1][:, :]), waits=[ev_exp])
    _v(lambda v: v.tensor_mul(numt[:, :], e1t[:, :], par["aa"][:, :]))
    _v(lambda v: v.tensor_mul(t1t[:, :], e2t[:, :], v0t[:, :]), fence=True)
    _v(lambda v: v.tensor_add(numt[:, :], numt[:, :], t1t[:, :]), fence=True)
    _v(lambda v: v.scalar_tensor_tensor(dent[:, :], e1t[:, :], 0.0,
                                        par["bb"][:, :], ALU.add, ALU.mult))
    _v(lambda v: v.tensor_add(dent[:, :], dent[:, :], e2t[:, :]), fence=True)
    _v(lambda v: v.reciprocal(t2t[:, :], dent[:, :]), fence=True)
    _v(lambda v: v.tensor_mul(wk0t[:, :], numt[:, :], t2t[:, :]), fence=True)
    _v(lambda v: v.tensor_mul(wk0t[:, :], wk0t[:, :], s0t[:, :]), fence=True)

    def fscat(v):
        last = None
        for oc in range(16):
            last = v.tensor_copy(B[:, oc * 2048:oc * 2048 + 1], wk0t[:, oc:oc + 1])
        return last
    col0_scatter = _v(fscat, fence=True)

    woq_free = [None, None]
    ystg_free = [None, None]
    ystore = {}
    natWo_free = None
    last_out_group = None
    woq_free[0] = last_attn_group
    woq_free[1] = last_attn_group
    out_evac_last = [None, None]
    for oq in range(4):
        qp = oq % 2
        ev_qt2 = [None, None]
        for tt in range(4):
            nb = tt % 2
            evw = S.step("g", ld(dqo[nb][:, :],
                                 wo_d[(oq * 4 + tt) * 128:(oq * 4 + tt + 1) * 128, :]),
                         waits=[dq_free[nb]], inc=g_dq[nb], n=16)
            dst = ap3(Ct, CW, woq_col[qp] + tt * 128, [[512, 16], [1, 128]])
            ev_qt2[nb] = S.step("s", lambda s_, nb=nb, dst=dst: s_.dma_start(
                out=dst, in_=dqo[nb][:, :], transpose=True),
                waits=[evw, woq_free[qp] if tt == 0 else None],
                inc=s_qt if nb == 0 else s_qtb, n=16)
            dq_free[nb] = ev_qt2[nb]
        ev_qt = ev_qt2
        order = list(range(1, 16)) + [0]
        for gi, ncl in enumerate(order):
            p = gi % 2

            def fo_(t, ncl=ncl, qp=qp, p=p):
                for j in range(CJ):
                    last = t.matmul(P[p][:, :],
                                    B[:, j * 2048 + ncl * 128:j * 2048 + ncl * 128 + 128],
                                    Ct[:, woq_col[qp] + j * 512:woq_col[qp] + (j + 1) * 512],
                                    start=(j == 0), stop=(j == CJ - 1))
                return last
            ev_g = S.step("t", fo_,
                          waits=ev_qt + [rwkv_last if ncl != 0 else col0_scatter,
                                         ystg_free[p]], inc=st)
            last_out_group = ev_g
            ev_e = S.step("v", lambda v, p=p: v.tensor_copy(
                E[1][:, p * 512:(p + 1) * 512], P[p][:, :]), waits=[ev_g], inc=sv)
            out_evac_last[p] = ev_e
            ev_s = S.step("s", lambda s_, p=p, ncl=ncl, oq=oq: s_.dma_start(
                out=yscr_d[ncl * 128:(ncl + 1) * 128, oq * 512:(oq + 1) * 512],
                in_=E[1][:, p * 512:(p + 1) * 512]), waits=[ev_e], inc=s_y[p], n=16)
            ystg_free[p] = ev_s
            ystore[(oq, ncl)] = ev_s
        woq_free[qp] = last_out_group
        for _ in range(8):
            emit_prep_unit()
    while _pi[0] < len(prep_units):
        emit_prep_unit()
    last_ystore = tot(s_y[0], s_y[1])
    ev_prep_done = tot(s_stb[0], s_stb[1])

    S.step("g", ld(Ct[0:16, bupB_col:bupB_col + 8192], upB_d),
           waits=[last_out_group], inc=g_lo, n=16)
    S.step("g", ld(Ct[0:16, bdnB_col:bdnB_col + 2048], dnB_d), inc=g_lo, n=16)
    S.step("g", ld(Ct[:, adn_col:adn_col + 1024],
                   bass.AP(dnA_d.tensor, 0, [[16, 128], [2048, 64], [1, 16]]), slow=False),
           inc=g_lo, n=16)
    S.step("g", ld(Ct[:, aup_col:aup_col + 256],
                   bass.AP(upA_d.tensor, 0, [[16, 128], [2048, 16], [1, 16]])),
           inc=g_lo, n=16)
    ev_lora = S.step("g", ld(Ct[0:1, bdnv_col:bdnv_col + 2048],
                             bass.AP(bdn_d.tensor, 0, [[2048, 1], [1, 2048]])),
                     inc=g_lo, n=16)
    ev_ones = S.step("v", lambda v: v.memset(Ct[0:1, ones_col:ones_col + 128], 1.0),
                     waits=[last_out_group], inc=sv)

    def ln_chain(tile_t, slot, dump, waits_data):
        t = lnT[slot]
        sy_, sq_, mn_, vr_, nv_, sd_, rs_ = [x[:, :] for x in t]
        s4 = lnS4[slot]
        tile = tile_t[:, :]
        ev1 = S.step("v", lambda v: v.reduce_sum(sy_, tile, AX.X),
                     waits=list(waits_data), inc=sv)

        def fsq(v):
            last = None
            for i in range(4):
                last = v.scalar_tensor_tensor(
                    dump[:, :], tile_t[:, i * 512:(i + 1) * 512], 0.0,
                    tile_t[:, i * 512:(i + 1) * 512], ALU.add, ALU.mult,
                    accum_out=s4[:, i:i + 1])
            return last
        ev2 = S.step("v", fsq, waits=list(waits_data), inc=sv)
        S.step("v", lambda v: v.reduce_sum(sq_, s4[:, :], AX.X), waits=[ev2], inc=sv)
        S.step("v", lambda v: v.tensor_scalar(mn_, sy_, INV_C, None, ALU.mult),
               waits=[(sv, sv.n)], inc=sv)
        S.step("v", lambda v: v.tensor_scalar(vr_, sq_, INV_C, None, ALU.mult),
               waits=[(sv, sv.n)], inc=sv)
        ev5 = S.step("v", lambda v: v.scalar_tensor_tensor(nv_, mn_, mn_, vr_,
                                                           ALU.mult, ALU.subtract),
                     waits=[(sv, sv.n)], inc=sv)
        ev6 = S.step("c", lambda c_: c_.activation(sd_, nv_, AF.Sqrt,
                                                   bias=epst[:, :], scale=-1.0),
                     waits=[ev5], inc=sc)
        S.step("v", lambda v: v.reciprocal(rs_, sd_), waits=[ev6], inc=sv)
        ev8 = S.step("v", lambda v: v.tensor_scalar(tile, tile, mn_, rs_,
                                                    ALU.subtract, ALU.mult),
                     waits=[(sv, sv.n)], inc=sv)
        return ev8, ev2

    yln_free = [[], []]
    E_free = [None, None]
    hbf_free = [None, None]
    early_ul = {}
    hstore = [None] * NCK
    hT_done = [None] * NCK
    prep_all_done = (sv, sv.n)
    last_sq = None
    for ncl in range(NCK):
        b = ncl % 2
        ev_y = S.step("g", ld(yln[b][:, :], yscr_d[ncl * 128:(ncl + 1) * 128, :]),
                      waits=last_ystore + [prep_all_done] + yln_free[b],
                      inc=g_yl[b], n=16)
        ev_x = S.step("g", ld(E[b][:, :], x_d[ncl * 128:(ncl + 1) * 128, :]),
                      waits=[E_free[b]], inc=g_E[b], n=16)
        ev_add = S.step("v", lambda v, b=b: v.tensor_add(yln[b][:, :], yln[b][:, :],
                                                         E[b][:, :]),
                        waits=[ev_y, ev_x], inc=sv)
        E_free[b] = ev_add
        ev_ap, ev_sq = ln_chain(yln[b], b, P[4], [ev_add, last_sq])
        last_sq = ev_sq
        hstore[ncl] = S.step("s", lambda s_, b=b, ncl=ncl: s_.dma_start(
            out=hscr_d[ncl * 128:(ncl + 1) * 128, :], in_=yln[b][:, :]),
            waits=[ev_ap], inc=s_hst[b], n=16)
        ev_cv = S.step("v", lambda v, b=b: v.tensor_copy(
            Ct[:, b * 2048:(b + 1) * 2048], yln[b][:, :]),
            waits=[ev_ap, hbf_free[b]], inc=sv)
        hT_done[ncl] = S.step("s", lambda s_, b=b, ncl=ncl: s_.dma_start(
            out=A[:, :, ncl * 128:(ncl + 1) * 128],
            in_=Ct[:, b * 2048:(b + 1) * 2048], transpose=True),
            waits=[ev_cv], inc=s_hT[b], n=16)
        hbf_free[b] = hT_done[ncl]
        yln_free[b] = [hstore[ncl], ev_cv]
        if ncl == 2:
            for hcq in (0, 1):
                early_ul[hcq] = S.step(
                    "g", ld(dqo[hcq][:, :], upscr_d[hcq * 128:(hcq + 1) * 128, :]),
                    waits=list(ev_prep_done) + [dq_free[hcq]], inc=g_dq[hcq], n=16)
    ev_hT_all = tot(s_hT[0], s_hT[1])
    ev_hst_all = tot(s_hst[0], s_hst[1])

    slabL_free = [None, None]
    relu_done = [[None] * HCN for _ in range(4)]
    x2_reader = [None] * HCN
    pU = [P[0], P[1]]
    down_evac = {}
    down_sq = {}
    wdt_free = [None, None]
    bank_guard = {i: None for i in range(8)}
    bank_guard[4] = last_sq

    for s_ in range(4):
        def ftu(t, s_=s_):
            last = None
            for j in range(CJ):
                last = t.matmul(P[3][0:16, :],
                                Ct[:, aup_col + j * 16:aup_col + (j + 1) * 16],
                                Asl(j, s_ * 512, 512),
                                start=(j == 0), stop=(j == CJ - 1))
            return last
        wtu = [hT_done[4 * s_ + 2], hT_done[4 * s_ + 3], ev_lora]
        if s_ > 0:
            wtu.append(down_evac[(s_ - 1, 1, 0, 3)])
        ev_tu = S.step("t", ftu, waits=wtu, inc=st)
        ev_tue = S.step("v", lambda v: v.tensor_copy(Ct[0:16, tup_col:tup_col + 512],
                                                     P[3][0:16, :]),
                        waits=[ev_tu], inc=sv)
        ev_tdn_last = None
        for hc in range(HCN):
            b = hc % 2
            if s_ == 0 and hc < 2:
                ev_l = early_ul[hc]
            else:
                wl = list(ev_prep_done) + [slabL_free[b]]
                ev_l = S.step("g", ld(dqo[b][:, :], upscr_d[hc * 128:(hc + 1) * 128, :]),
                              waits=wl, inc=g_dq[b], n=16)

            def fu(t, b=b, hc=hc, s_=s_):
                for j in range(CJ):
                    t.matmul(pU[hc % 2][:, :], dqo[b][:, j * 128:(j + 1) * 128],
                             Asl(j, s_ * 512, 512), start=(j == 0), stop=False)
                return t.matmul(pU[hc % 2][:, :],
                                Ct[0:16, bupB_col + hc * 128:bupB_col + hc * 128 + 128],
                                Ct[0:16, tup_col:tup_col + 512],
                                start=False, stop=True)
            wu = [ev_l, ev_tue, hT_done[4 * s_ + 2], hT_done[4 * s_ + 3]]
            if hc >= 2:
                wu.append(relu_done[s_][hc - 2])
            if s_ > 0:
                wu.append(x2_reader[hc])
                if hc < 2:
                    wu.append(down_evac[(s_ - 1, 1, 0, hc % 2)])
            elif hc < 2:
                wu.append(out_evac_last[hc % 2])
            if bank_guard.get(hc % 2) is not None:
                wu.append(bank_guard[hc % 2])
            ev_u = S.step("t", fu, waits=wu, inc=st)
            slabL_free[b] = ev_u
            relu_done[s_][hc] = S.step("c", lambda c_, hc=hc: c_.activation(
                B[:, hc * 512:(hc + 1) * 512], pU[hc % 2][:, :], AF.Relu,
                bias=bup_t[:, hc:hc + 1]), waits=[ev_u], inc=sc)

            def emit_tdn(hcx):
                def ftd(t, hcx=hcx):
                    return t.matmul(P[2][0:16, :],
                                    Ct[:, adn_col + hcx * 16:adn_col + (hcx + 1) * 16],
                                    B[:, hcx * 512:(hcx + 1) * 512],
                                    start=(hcx == 0), stop=(hcx == HCN - 1))
                wtd = [relu_done[s_][hcx]]
                if hcx == 0:
                    if s_ > 0:
                        wtd.append(down_evac[(s_ - 1, 1, 0, 2)])
                    if bank_guard.get(2) is not None:
                        wtd.append(bank_guard[2])
                return S.step("t", ftd, waits=wtd, inc=st)
            if hc >= 1:
                emit_tdn(hc - 1)
            if hc == HCN - 1:
                ev_tdn_last = emit_tdn(hc)
        ev_tde = S.step("v", lambda v: v.tensor_copy(Ct[0:16, tdn_col:tdn_col + 512],
                                                     P[2][0:16, :]),
                        waits=[ev_tdn_last], inc=sv)
        for pair in range(2):
            ev_h = [None, None]
            for l in range(2):
                ncl = s_ * 4 + pair * 2 + l
                wE = list(ev_hst_all) + [E_free[l]]
                if pair == 1:
                    wE.append(down_evac[(s_, 0, l, 3)])
                elif s_ > 0:
                    wE.append(down_evac[(s_ - 1, 1, l, 3)])
                ev_h[l] = S.step("g", ld(E[l][:, :], hscr_d[ncl * 128:(ncl + 1) * 128, :]),
                                 waits=wE, inc=g_E[l], n=16)
            for hc in range(HCN):
                wb = hc % 2
                wwd = list(ev_prep_done) + [wdt_free[wb], last_out_group]
                if s_ == 0 and pair == 0 and hc < 2:
                    wwd += ev_hT_all
                ev_w = S.step("c",
                              lambda e_, wb=wb, hc=hc: e_.dma_start(
                    out=Ct[:, wdt_col[wb]:wdt_col[wb] + 2048],
                    in_=dnscr_d[0:2048, hc * 128:(hc + 1) * 128], transpose=True),
                    waits=wwd, inc=s_wd[wb], n=16)

                def fd(t, hc=hc, pair=pair, wb=wb):
                    last = None
                    for l in range(2):
                        for dt in range(4):
                            last = t.matmul(
                                P[l * 4 + dt][:, :],
                                B[:, hc * 512 + (pair * 2 + l) * 128:
                                  hc * 512 + (pair * 2 + l) * 128 + 128],
                                Ct[:, wdt_col[wb] + dt * 512:wdt_col[wb] + (dt + 1) * 512],
                                start=(hc == 0), stop=False)
                    return last
                wd = [ev_w, relu_done[s_][hc]]
                if hc == 0:
                    if pair == 0:
                        wd += [relu_done[s_][HCN - 1], ev_tde]
                        for bk in range(8):
                            if bank_guard.get(bk) is not None:
                                wd.append(bank_guard[bk])
                    else:
                        wd += [down_evac[(s_, 0, l2, dt2)]
                               for l2 in range(2) for dt2 in range(4)]
                        wd += [down_sq.get((s_, 0, 0)), down_sq.get((s_, 0, 1))]
                ev_d = S.step("t", fd, waits=wd, inc=st)
                wdt_free[wb] = ev_d
                if pair == 1:
                    x2_reader[hc] = ev_d
            grp_done = {}
            for l in range(2):
                for dt in range(4):
                    def ff(t, l=l, dt=dt, pair=pair):
                        t.matmul(P[l * 4 + dt][:, :],
                                 Ct[0:16, tdn_col + (pair * 2 + l) * 128:
                                    tdn_col + (pair * 2 + l) * 128 + 128],
                                 Ct[0:16, bdnB_col + dt * 512:bdnB_col + (dt + 1) * 512],
                                 start=False, stop=False)
                        return t.matmul(P[l * 4 + dt][:, :],
                                        Ct[0:1, ones_col:ones_col + 128],
                                        Ct[0:1, bdnv_col + dt * 512:bdnv_col + (dt + 1) * 512],
                                        start=False, stop=True)
                    grp_done[(l, dt)] = S.step("t", ff, waits=[ev_ones, ev_lora], inc=st)
            for l in range(2):
                ncl = s_ * 4 + pair * 2 + l
                for dt in range(4):
                    def fe(v, l=l, dt=dt):
                        return v.tensor_add(yln[l][:, dt * 512:(dt + 1) * 512],
                                            P[l * 4 + dt][:, :],
                                            E[l][:, dt * 512:(dt + 1) * 512])
                    down_evac[(s_, pair, l, dt)] = S.step(
                        "v", fe, waits=[grp_done[(l, dt)], ev_h[l]] + yln_free[l],
                        inc=sv)
                ev_ap, ev_sq = ln_chain(yln[l], l, P[l * 4],
                                        [down_evac[(s_, pair, l, 3)]])
                down_sq[(s_, pair, l)] = ev_sq
                bank_guard[l * 4] = ev_sq
                ev_so = S.step("s", lambda s2, l=l, ncl=ncl: s2.dma_start(
                    out=out_d[ncl * 128:(ncl + 1) * 128, :], in_=yln[l][:, :]),
                    waits=[ev_ap], inc=s_out[l], n=16)
                yln_free[l] = [ev_so]
                E_free[l] = down_evac[(s_, pair, l, 3)]

    dma_sems = ([g_par, g_wo, g_lo] + g_dq + g_qi + g_sc + g_nw + g_E + g_yl
                + s_x + s_sl + s_ut + s_stb + [s_qt] + s_y + s_out + s_hst
                + s_hT + s_wd)
    S.step("s", None, waits=tot(*dma_sems))
    S.step("g", None, waits=tot(s_out[0], s_out[1]))

    S.check_deadlock()

    with es, nc.Block() as block:
        def runner(engname):
            def run(eng):
                for w, fn, inc, n in S.streams[engname]:
                    for sem, val in w:
                        eng.wait_ge(sem.h, val)
                    if fn is None:
                        continue
                    last = fn(eng)
                    if inc is not None:
                        last.then_inc(inc.h, n)
            return run
        block.gpsimd(runner("g"))
        block.sync(runner("s"))
        block.tensor(runner("t"))
        block.vector(runner("v"))
        block.scalar(runner("c"))
    return nc


def kernel(**inputs):
    x = np.asarray(inputs["L0_attn_x"])

    def vec(name):
        return np.ascontiguousarray(np.asarray(inputs[name]).reshape(-1))
    common = {
        "w_rec": np.ascontiguousarray(inputs["L0_attn_w_rec"]),
        "w_value": np.ascontiguousarray(inputs["L0_attn_w_value"]),
        "w_key": np.ascontiguousarray(inputs["L0_attn_w_key"]),
        "w_out": np.ascontiguousarray(inputs["L0_attn_w_out"]),
        "tf": vec("L0_attn_time_first"),
        "pp": vec("L0_attn_pp"),
        "aa": vec("L0_attn_aa"),
        "bb": vec("L0_attn_bb"),
        "tmr": vec("L0_attn_time_mix_r"),
        "tmv": vec("L0_attn_time_mix_v"),
        "tmk": vec("L0_attn_time_mix_k"),
        "xx": vec("L0_attn_xx"),
        "w_up_q": np.ascontiguousarray(inputs["L0_ffn_w_up_q"]),
        "w_up_scale": np.ascontiguousarray(inputs["L0_ffn_w_up_scale"]),
        "b_up": vec("L0_ffn_b_up"),
        "up_lora_a": np.ascontiguousarray(inputs["L0_ffn_w_up_lora_a"]),
        "up_lora_b": np.ascontiguousarray(inputs["L0_ffn_w_up_lora_b"]),
        "w_down_q": np.ascontiguousarray(inputs["L0_ffn_w_down_q"]),
        "w_down_scale": np.ascontiguousarray(inputs["L0_ffn_w_down_scale"]),
        "b_down": vec("L0_ffn_b_down"),
        "dn_lora_a": np.ascontiguousarray(inputs["L0_ffn_w_down_lora_a"]),
        "dn_lora_b": np.ascontiguousarray(inputs["L0_ffn_w_down_lora_b"]),
    }
    nc = build()
    in_maps = [dict(common, x=np.ascontiguousarray(x[i])) for i in range(8)]
    res = run_bass_kernel_spmd(nc, in_maps, core_ids=list(range(8)))
    return np.stack([np.asarray(r["out"], dtype=np.float32) for r in res.results],
                    axis=0)


# version 13
# speedup vs baseline: 1.1226x; 1.0972x over previous
import numpy as np
from contextlib import ExitStack
import concourse.bass as bass
import concourse.mybir as mybir
from concourse.bass_utils import run_bass_kernel_spmd

F32 = mybir.dt.float32
BF16 = mybir.dt.bfloat16
I32 = mybir.dt.int32
AF = mybir.ActivationFunctionType
ALU = mybir.AluOpType
AX = mybir.AxisListType

TOK = 2048
C = 2048
H = 8192
CJ = 16
NT = 4
NCK = 16
HCN = 64
INV_C = 1.0 / 2048.0
FULL_K = True

ENGS = ("g", "s", "t", "v", "c")


class _Sem:
    __slots__ = ("name", "n", "h")

    def __init__(self, name):
        self.name, self.n, self.h = name, 0, None


class Sched:
    def __init__(self):
        self.streams = {e: [] for e in ENGS}
        self.sems = []
        self.wm = {e: {} for e in ENGS}

    def sem(self, name):
        s = _Sem(name)
        self.sems.append(s)
        return s

    def step(self, eng, fn, waits=(), inc=None, n=1):
        w = []
        for item in waits:
            if item is None:
                continue
            sem, val = item
            if val is None or val <= 0:
                continue
            if self.wm[eng].get(sem.name, 0) >= val:
                continue
            self.wm[eng][sem.name] = val
            w.append((sem, val))
        ev = None
        if inc is not None:
            inc.n += n
            ev = (inc, inc.n)
        self.streams[eng].append((w, fn, inc, n))
        return ev

    def check_deadlock(self):
        ptr = {e: 0 for e in ENGS}
        val = {s.name: 0 for s in self.sems}
        progress = True
        while progress:
            progress = False
            for e in ENGS:
                while ptr[e] < len(self.streams[e]):
                    w, fn, inc, n = self.streams[e][ptr[e]]
                    if any(val[s.name] < v for s, v in w):
                        break
                    if inc is not None:
                        val[inc.name] += n
                    ptr[e] += 1
                    progress = True
        stuck = {e: (ptr[e], len(self.streams[e])) for e in ENGS
                 if ptr[e] < len(self.streams[e])}
        if stuck:
            det = []
            for e, (p, tot) in stuck.items():
                w = self.streams[e][p][0]
                bad = [(s.name, v, val[s.name]) for s, v in w if val[s.name] < v]
                det.append(f"{e}: step {p}/{tot} waits {bad}")
            raise RuntimeError("schedule deadlock: " + "; ".join(det))


def build():
    nc = bass.Bass("TRN2", target_bir_lowering=False)

    x_d = nc.dram_tensor("x", [TOK, C], F32, kind="ExternalInput").ap()
    wm_d = [nc.dram_tensor(nm, [C, C], F32, kind="ExternalInput").ap()
            for nm in ("w_rec", "w_value", "w_key")]
    wo_d = nc.dram_tensor("w_out", [C, C], F32, kind="ExternalInput").ap()
    vecs = {nm: nc.dram_tensor(nm, [C], F32, kind="ExternalInput").ap()
            for nm in ("tf", "pp", "aa", "bb", "tmr", "tmv", "tmk", "xx")}
    upq_d = nc.dram_tensor("w_up_q", [H, C], I32, kind="ExternalInput").ap()
    upsc_d = nc.dram_tensor("w_up_scale", [H, 32], F32, kind="ExternalInput").ap()
    bup_d = nc.dram_tensor("b_up", [H], F32, kind="ExternalInput").ap()
    upA_d = nc.dram_tensor("up_lora_a", [C, 16], F32, kind="ExternalInput").ap()
    upB_d = nc.dram_tensor("up_lora_b", [16, H], F32, kind="ExternalInput").ap()
    dnq_d = nc.dram_tensor("w_down_q", [C, H], I32, kind="ExternalInput").ap()
    dnsc_d = nc.dram_tensor("w_down_scale", [C, 128], F32, kind="ExternalInput").ap()
    bdn_d = nc.dram_tensor("b_down", [C], F32, kind="ExternalInput").ap()
    dnA_d = nc.dram_tensor("dn_lora_a", [H, 16], F32, kind="ExternalInput").ap()
    dnB_d = nc.dram_tensor("dn_lora_b", [16, C], F32, kind="ExternalInput").ap()
    out_d = nc.dram_tensor("out", [TOK, C], F32, kind="ExternalOutput").ap()
    upscr_d = nc.dram_tensor("up_scr", [H, 2048], BF16).ap()
    dnscr_d = nc.dram_tensor("dn_scr", [C, H], BF16).ap()
    yscr_d = nc.dram_tensor("y_scr", [TOK, C], F32).ap()
    hscr_d = nc.dram_tensor("h_scr", [TOK, C], F32).ap()

    es = ExitStack()

    SBB = 16512

    def sb(name, shape, dtype, off):
        return nc.alloc_sbuf_tensor_at(name, shape, dtype, offset=SBB + off)

    A = sb("A", [128, 16, 2048], BF16, 0)
    B = sb("B", [128, 32768], BF16, 65536)
    CW = 18816
    Ct = sb("Cx", [128, CW], BF16, 131072)
    DBASE = 131072 + 37632
    qin = [sb(f"qin{i}", [128, 2048], I32, DBASE + i * 8192) for i in range(2)]
    dqo = [sb(f"dqo{i}", [128, 2048], BF16, DBASE + 16384 + i * 4096) for i in range(2)]
    yln = [sb(f"yln{i}", [128, 2048], F32, DBASE + i * 8192) for i in range(2)]
    EBASE = DBASE + 24576
    E = [sb(f"Ee{i}", [128, 2048], F32, EBASE + i * 8192) for i in range(2)]
    FBASE = EBASE + 16384
    fo = [FBASE]

    def fsb(name, shape, dtype):
        t = sb(name, shape, dtype, fo[0])
        width = 1
        for d in shape[1:]:
            width *= d
        fo[0] += (width * mybir.dt.size(dtype) + 31) // 32 * 32
        assert SBB + fo[0] <= 229344, f"F overflow at {name}: {fo[0]}"
        return t

    par = {nm: fsb("p_" + nm, [128, 16], F32) for nm in
           ("tf", "pp", "aa", "bb", "tmr", "tmv", "tmk", "xx")}
    cmf = [fsb(f"cmf{m}", [128, 16], F32) for m in range(3)]
    cmb = [fsb(f"cmb{m}", [128, 16], BF16) for m in range(3)]
    biasSB = [fsb(f"biasSB{m}", [128, 16], F32) for m in range(3)]
    bup_t = fsb("bup", [128, 64], F32)
    upsct = [fsb(f"upsct{i}", [128, 32], F32) for i in range(2)]
    dnsct = [fsb(f"dnsct{i}", [128, 32], F32) for i in range(2)]
    epst = fsb("epst", [128, 1], F32)
    k0t = fsb("k0t", [128, 16], F32)
    v0t = fsb("v0t", [128, 16], F32)
    s0t = fsb("s0t", [128, 16], F32)
    wwt = fsb("wwt", [128, 16], F32)
    qqt = fsb("qqt", [128, 16], F32)
    t1t = fsb("t1t", [128, 16], F32)
    t2t = fsb("t2t", [128, 16], F32)
    e1t = fsb("e1t", [128, 16], F32)
    e2t = fsb("e2t", [128, 16], F32)
    numt = fsb("numt", [128, 16], F32)
    dent = fsb("dent", [128, 16], F32)
    wk0t = fsb("wk0t", [128, 16], F32)
    lnT = [[fsb(f"ln{n}{i}", [128, 1], F32) for n in
            ("sy", "sq", "mn", "vr", "nv", "sd", "rs")] for i in range(2)]
    lnS4 = [fsb(f"lns4{i}", [128, 4], F32) for i in range(2)]

    slab_col = lambda m, p: (m * 2 + p) * 2048
    natW_col = [12288, 14336]
    upstg_col = 16384
    natWo_col = 16384
    woq_col = [0, 8192]
    wdt_col = [0, 2048]
    bupB_col = 4096
    bdnB_col = 12288
    adn_col = 14336
    aup_col = 15360
    bdnv_col = 15616
    ones_col = 17664
    tup_col = 17792
    tdn_col = 18304

    P = [es.enter_context(nc.psum_tensor(f"pb{i}", [128, 512], F32)) for i in range(8)]

    S = Sched()
    st = S.sem("st")
    sv = S.sem("sv")
    sc = S.sem("sc")
    g_par = S.sem("g_par")
    g_wo = S.sem("g_wo")
    g_lo = S.sem("g_lo")
    g_dq = [S.sem(f"g_dq{i}") for i in range(2)]
    g_qi = [S.sem(f"g_qi{i}") for i in range(2)]
    g_sc = [S.sem(f"g_sc{i}") for i in range(2)]
    g_nw = [S.sem(f"g_nw{i}") for i in range(2)]
    g_E = [S.sem(f"g_E{i}") for i in range(2)]
    g_yl = [S.sem(f"g_yl{i}") for i in range(2)]
    s_x = [S.sem(f"s_x{i}") for i in range(2)]
    s_sl = [S.sem(f"s_sl{i}") for i in range(6)]
    s_ut = [S.sem(f"s_ut{i}") for i in range(2)]
    s_stb = [S.sem(f"s_stb{i}") for i in range(2)]
    s_qt = S.sem("s_qt")
    s_qtb = S.sem("s_qtb")
    s_y = [S.sem(f"s_y{i}") for i in range(2)]
    s_out = [S.sem(f"s_out{i}") for i in range(2)]
    s_hst = [S.sem(f"s_hst{i}") for i in range(2)]
    s_hT = [S.sem(f"s_hT{i}") for i in range(2)]
    s_wd = [S.sem(f"s_wd{i}") for i in range(2)]
    for sem in S.sems:
        sem.h = es.enter_context(nc.semaphore(sem.name))

    def tot(*sems):
        return [(s, s.n) for s in sems]

    def ap2(tens, width, off_el, w):
        return bass.AP(tens, off_el, [[width, 128]] + [[1, w]])

    def ap3(tens, width, off_el, dims):
        return bass.AP(tens, off_el, [[width, 128]] + dims)

    def Asl(j, a, w):
        return bass.AP(A, j * 2048 + a, [[32768, 128], [1, w]])

    def ld(dst_ap, src_ap, slow=False):
        def fn(g, d=dst_ap, s_=src_ap, sl=slow):
            if sl:
                return g.dma_start(out=d, in_=s_, allow_slow_non_contiguous=True)
            return g.dma_start(out=d, in_=s_)
        return fn

    ev_par = None
    for nm, t in par.items():
        src = bass.AP(vecs[nm].tensor, 0, [[1, 128], [128, 16]])
        ev_par = S.step("g", ld(t[:, :], src, slow=True), inc=g_par, n=16)
    ev_par = S.step("g", ld(bup_t[:, :],
                            bass.AP(bup_d.tensor, 0, [[1, 128], [128, 64]]), slow=True),
                    inc=g_par, n=16)
    S.step("v", lambda v: v.memset(epst[:, :], 1e-5), inc=sv)
    tm_names = ("tmr", "tmv", "tmk")
    ev_cmb = [None] * 3
    for m in range(3):
        tmt = par[tm_names[m]]
        ev = S.step("v", lambda v, m=m, tmt=tmt: v.tensor_scalar(
            cmf[m][:, :], tmt[:, :], -1.0, 1.0, ALU.mult, op1=ALU.add),
            waits=[ev_par], inc=sv)
        ev = S.step("v", lambda v, m=m: v.tensor_mul(
            cmf[m][:, :], cmf[m][:, :], par["xx"][:, :]),
            waits=[(sv, sv.n)], inc=sv)
        ev_cmb[m] = S.step("v", lambda v, m=m: v.tensor_copy(
            cmb[m][:, :], cmf[m][:, :]), waits=[(sv, sv.n)], inc=sv)

    ev_xtr = [None] * 16
    dq_free = [None, None]
    for tt in range(16):
        ev_l = S.step("g", ld(dqo[tt % 2][:, :], x_d[tt * 128:(tt + 1) * 128, :]),
                      waits=[dq_free[tt % 2]], inc=g_dq[tt % 2], n=16)
        ev_xtr[tt] = S.step("s", lambda s_, tt=tt: s_.dma_start(
            out=A[:, :, tt * 128:(tt + 1) * 128], in_=dqo[tt % 2][:, :], transpose=True),
            waits=[ev_l], inc=s_x[tt % 2], n=16)
        dq_free[tt % 2] = ev_xtr[tt]
    ev_xtr_all = tot(s_x[0], s_x[1])

    prep_units = [("u", hc) for hc in range(HCN)]
    for q in range(4):
        for dc in range(16):
            prep_units.append(("d", dc, q))
    qin_free = [None, None]
    upscr_done = [None] * HCN
    dn_qdone = [None] * 4
    last_upstg = [None]
    _pi = [0]

    def emit_prep_unit():
        i = _pi[0]
        if i >= len(prep_units):
            return
        _pi[0] += 1
        u = prep_units[i]
        b = i % 2
        if u[0] == "u":
            hc = u[1]
            evq = S.step("c", ld(qin[b][:, :], upq_d[hc * 128:(hc + 1) * 128, :]),
                         waits=[qin_free[b]], inc=g_qi[b], n=16)
            evs = S.step("c", ld(upsct[b][:, :], upsc_d[hc * 128:(hc + 1) * 128, :]),
                         inc=g_sc[b], n=16)
            scb = bass.AP(upsct[b], 0, [[32, 128], [1, 32], [0, 64]])
            evd = S.step("v", lambda v, b=b, scb=scb: v.scalar_tensor_tensor(
                dqo[b][:, :], qin[b][:, :], -7.5, scb, ALU.add, ALU.mult),
                waits=[evq, evs, dq_free[b]], inc=sv)
            qin_free[b] = evd
            dst = ap3(Ct, CW, upstg_col, [[128, 16], [1, 128]])
            evtp = S.step("s", lambda s_, b=b, dst=dst: s_.dma_start(
                out=dst, in_=dqo[b][:, :], transpose=True), waits=[evd],
                inc=s_ut[b], n=16)
            evst = S.step("s", lambda s_, hc=hc: s_.dma_start(
                out=upscr_d[hc * 128:(hc + 1) * 128, :],
                in_=Ct[:, upstg_col:upstg_col + 2048]), waits=[evtp],
                inc=s_stb[b], n=16)
            dq_free[b] = evst
            upscr_done[hc] = evst
            last_upstg[0] = evst
        else:
            dc, q = u[1], u[2]
            evq = S.step("c", ld(qin[b][:, :],
                                 dnq_d[dc * 128:(dc + 1) * 128, q * 2048:(q + 1) * 2048]),
                         waits=[qin_free[b]], inc=g_qi[b], n=16)
            evs = S.step("c", ld(dnsct[b][:, :],
                                 dnsc_d[dc * 128:(dc + 1) * 128, q * 32:(q + 1) * 32]),
                         inc=g_sc[b], n=16)
            scb = bass.AP(dnsct[b], 0, [[32, 128], [1, 32], [0, 64]])
            evd = S.step("v", lambda v, b=b, scb=scb: v.scalar_tensor_tensor(
                dqo[b][:, :], qin[b][:, :], -7.5, scb, ALU.add, ALU.mult),
                waits=[evq, evs, dq_free[b]], inc=sv)
            qin_free[b] = evd
            evst = S.step("s", lambda s_, dc=dc, q=q: s_.dma_start(
                out=dnscr_d[dc * 128:(dc + 1) * 128, q * 2048:(q + 1) * 2048],
                in_=dqo[b][:, :]), waits=[evd], inc=s_stb[b], n=16)
            dq_free[b] = evst
            dn_qdone[q] = evst

    natW_free = [None, None]
    slab_free = [[None, None] for _ in range(3)]
    bias_bank_free = [None, None]
    sig_free = [None, None, None, None]
    bankR_free = [None, None]
    bankV_free = [None, None]
    bankK_free = [None]
    rwkv_last = None
    pR, pV, pK, pBias = [P[0], P[1]], [P[2], P[3]], [P[4], P[5]], [P[6], P[7]]

    ev_tr_k = {}
    ev_bias_k = {}
    ev_scale_k = {}

    def stage_prep(oc):
        ev_tr = [None] * 3
        for m in range(3):
            nb = (oc * 3 + m) % 2
            evw = S.step("g", ld(Ct[:, natW_col[nb]:natW_col[nb] + 2048],
                                 wm_d[m][oc * 128:(oc + 1) * 128, :]),
                         waits=[natW_free[nb]], inc=g_nw[nb], n=16)
            dst = ap3(Ct, CW, slab_col(m, oc % 2), [[128, 16], [1, 128]])
            ev_tr[m] = S.step("s", lambda s_, nb=nb, dst=dst: s_.dma_start(
                out=dst, in_=Ct[:, natW_col[nb]:natW_col[nb] + 2048], transpose=True),
                waits=[evw, slab_free[m][oc % 2]], inc=s_sl[m * 2 + oc % 2], n=16)
            natW_free[nb] = ev_tr[m]
        ev_tr_k[oc] = ev_tr

    def stage_bias(oc):
        ev_tr = ev_tr_k[oc]
        ev_biasg = [None] * 3
        for m in range(3):
            def fb(t, m=m, oc=oc):
                base = slab_col(m, oc % 2)
                for j in range(CJ):
                    last = t.matmul(pBias[oc % 2][:, m:m + 1],
                                    Ct[:, base + j * 128:base + j * 128 + 128],
                                    cmb[m][:, j:j + 1],
                                    start=(j == 0), stop=(j == CJ - 1))
                return last
            ev_biasg[m] = S.step("t", fb,
                                 waits=[ev_tr[m], ev_cmb[m], bias_bank_free[oc % 2]],
                                 inc=st)
        ev_bias_k[oc] = ev_biasg

    def stage_scale(oc):
        ev_biasg = ev_bias_k[oc]

        def fbe(v, oc=oc):
            last = None
            for m in range(3):
                last = v.tensor_copy(biasSB[m][:, oc:oc + 1], pBias[oc % 2][:, m:m + 1])
            return last
        ev_bev = S.step("v", fbe, waits=[ev_biasg[2]], inc=sv)
        bias_bank_free[oc % 2] = ev_bev
        ev_scale = [None] * 3
        for m in range(3):
            def fs(v, m=m, oc=oc):
                base = slab_col(m, oc % 2)
                last = None
                for j in range(CJ):
                    last = v.tensor_scalar(
                        Ct[:, base + j * 128:base + j * 128 + 128],
                        Ct[:, base + j * 128:base + j * 128 + 128],
                        par[tm_names[m]][:, j:j + 1], None, ALU.mult)
                return last
            ev_scale[m] = S.step("v", fs, waits=[ev_biasg[m]], inc=sv)
        ev_scale_k[oc] = (ev_bev, ev_scale)

    def prep_m(oc, m):
        nb = (oc * 3 + m) % 2
        evw = S.step("g", ld(Ct[:, natW_col[nb]:natW_col[nb] + 2048],
                             wm_d[m][oc * 128:(oc + 1) * 128, :]),
                     waits=[natW_free[nb]], inc=g_nw[nb], n=16)
        dst = ap3(Ct, CW, slab_col(m, oc % 2), [[128, 16], [1, 128]])
        ev = S.step("s", lambda s_, nb=nb, dst=dst: s_.dma_start(
            out=dst, in_=Ct[:, natW_col[nb]:natW_col[nb] + 2048], transpose=True),
            waits=[evw, slab_free[m][oc % 2]], inc=s_sl[m * 2 + oc % 2], n=16)
        natW_free[nb] = ev
        ev_tr_k.setdefault(oc, [None] * 3)[m] = ev

    def stage_prep(oc):
        for m in range(3):
            prep_m(oc, m)

    def stage_main(oc):
        nonlocal rwkv_last, last_attn_group
        ev_bev, ev_scale = ev_scale_k[oc]
        groups = {}

        def emit_group(mi, nt, bank, bfree):
            def fg(t, mi=mi, oc=oc, nt=nt, bank=bank):
                base = slab_col(mi, oc % 2)
                p = nt % 2
                for j in range(CJ):
                    last = t.matmul(bank[p][:, :],
                                    Ct[:, base + j * 128:base + j * 128 + 128],
                                    Asl(j, nt * 512, 512),
                                    start=(j == 0), stop=(j == CJ - 1))
                return last
            waits = [ev_scale[mi]] + ev_xtr_all
            if bfree is not None:
                waits.append(bfree)
            return S.step("t", fg, waits=waits, inc=st)

        ev_sigs = [None] * NT

        def emit_sig(nt):
            ev_sigs[nt] = S.step("c", lambda c_, nt=nt, oc=oc: c_.activation(
                E[0][:, nt * 512:(nt + 1) * 512], pR[nt % 2][:, :], AF.Sigmoid,
                bias=biasSB[0][:, oc:oc + 1]),
                waits=[groups[("r", nt)], ev_bev, sig_free[nt]], inc=sc)
            bankR_free[nt % 2] = ev_sigs[nt]
        for nt in (0, 1):
            groups[("r", nt)] = emit_group(0, nt, pR, bankR_free[nt % 2])
        for nt in (0, 1):
            emit_sig(nt)
        for nt in (2, 3):
            groups[("r", nt)] = emit_group(0, nt, pR, bankR_free[nt % 2])
        for nt in (2, 3):
            emit_sig(nt)
        slab_free[0][oc % 2] = groups[("r", NT - 1)]
        if oc + 1 < 16:
            stage_bias(oc + 1)
        prep_m(oc + 2, 0) if oc + 2 < 16 else None
        ev_rws = [None] * NT

        def emit_rwkv(nt):
            c0 = 1 if nt == 0 else 0

            def fr(v, oc=oc, nt=nt, c0=c0):
                p = nt % 2
                return v.scalar_tensor_tensor(
                    B[:, oc * 2048 + nt * 512 + c0:oc * 2048 + (nt + 1) * 512],
                    pV[p][:, c0:512], biasSB[1][:, oc:oc + 1],
                    E[0][:, nt * 512 + c0:(nt + 1) * 512],
                    ALU.add, ALU.mult)
            ev_rws[nt] = S.step("v", fr, waits=[groups[("v", nt)], ev_sigs[nt]],
                                inc=sv)
            bankV_free[nt % 2] = ev_rws[nt]
            sig_free[nt] = ev_rws[nt]
            if nt == 0:
                def f0v(v, oc=oc):
                    return v.tensor_copy(v0t[:, oc:oc + 1], pV[0][:, 0:1])
                ev_f0v = S.step("v", f0v, waits=[groups[("v", 0)]], inc=sv)
                bankV_free[0] = ev_f0v
        for nt in (0, 1):
            groups[("v", nt)] = emit_group(1, nt, pV, bankV_free[nt % 2])
        for nt in (0, 1):
            emit_rwkv(nt)
        for nt in (2, 3):
            groups[("v", nt)] = emit_group(1, nt, pV, bankV_free[nt % 2])
        for nt in (2, 3):
            emit_rwkv(nt)
        slab_free[1][oc % 2] = groups[("v", NT - 1)]
        rwkv_last = ev_rws[NT - 1]
        prep_m(oc + 2, 1) if oc + 2 < 16 else None
        ev_c0 = None
        for nt in range(NT):
            if nt == 2:
                def f0k(v, oc=oc):
                    v.tensor_copy(k0t[:, oc:oc + 1], pK[0][:, 0:1])
                    return v.tensor_copy(s0t[:, oc:oc + 1], E[0][:, 0:1])
                ev_c0 = S.step("v", f0k, waits=[groups[("k", 0)], ev_sigs[0]],
                               inc=sv)
                bankK_free[0] = ev_c0
            bfree = bankK_free[0] if nt == 2 else None
            groups[("k", nt)] = emit_group(2, nt, pK, bfree)
        slab_free[2][oc % 2] = groups[("k", NT - 1)]
        last_attn_group = groups[("k", NT - 1)]
        sig_free[0] = ev_c0
        rwkv_last = ev_c0
        prep_m(oc + 2, 2) if oc + 2 < 16 else None
        if oc + 1 < 16:
            stage_scale(oc + 1)

    last_attn_group = None
    stage_prep(0)
    stage_bias(0)
    stage_scale(0)
    stage_prep(1)
    for oc in range(16):
        stage_main(oc)
        for _ in range(8):
            emit_prep_unit()

    col0_holder = [None]

    def emit_col0():
        def _v(fn, waits=(), fence=False):
            w = list(waits)
            if fence:
                w.append((sv, sv.n))
            return S.step("v", fn, waits=w, inc=sv)

        _v(lambda v: v.scalar_tensor_tensor(wwt[:, :], k0t[:, :], 0.0,
                                            biasSB[2][:, :], ALU.add, ALU.add))
        _v(lambda v: v.tensor_add(wwt[:, :], wwt[:, :], par["tf"][:, :]), fence=True)
        _v(lambda v: v.tensor_max(qqt[:, :], par["pp"][:, :], wwt[:, :]), fence=True)
        _v(lambda v: v.tensor_sub(t1t[:, :], par["pp"][:, :], qqt[:, :]), fence=True)
        ev = _v(lambda v: v.tensor_sub(t2t[:, :], wwt[:, :], qqt[:, :]), fence=True)

        def fexp(c_):
            c_.activation(e1t[:, :], t1t[:, :], AF.Exp)
            return c_.activation(e2t[:, :], t2t[:, :], AF.Exp)
        ev_exp = S.step("c", fexp, waits=[ev], inc=sc)
        _v(lambda v: v.tensor_add(v0t[:, :], v0t[:, :], biasSB[1][:, :]), waits=[ev_exp])
        _v(lambda v: v.tensor_mul(numt[:, :], e1t[:, :], par["aa"][:, :]))
        _v(lambda v: v.tensor_mul(t1t[:, :], e2t[:, :], v0t[:, :]), fence=True)
        _v(lambda v: v.tensor_add(numt[:, :], numt[:, :], t1t[:, :]), fence=True)
        _v(lambda v: v.scalar_tensor_tensor(dent[:, :], e1t[:, :], 0.0,
                                            par["bb"][:, :], ALU.add, ALU.mult))
        _v(lambda v: v.tensor_add(dent[:, :], dent[:, :], e2t[:, :]), fence=True)
        _v(lambda v: v.reciprocal(t2t[:, :], dent[:, :]), fence=True)
        _v(lambda v: v.tensor_mul(wk0t[:, :], numt[:, :], t2t[:, :]), fence=True)
        _v(lambda v: v.tensor_mul(wk0t[:, :], wk0t[:, :], s0t[:, :]), fence=True)

        def fscat(v):
            last = None
            for oc in range(16):
                last = v.tensor_copy(B[:, oc * 2048:oc * 2048 + 1], wk0t[:, oc:oc + 1])
            return last
        col0_scatter = _v(fscat, fence=True)

        col0_holder[0] = col0_scatter

    woq_free = [None, None]
    ystg_free = [None, None]
    ystore = {}
    natWo_free = None
    last_out_group = None
    woq_free[0] = last_attn_group
    woq_free[1] = last_attn_group
    out_evac_last = [None, None]
    for oq in range(4):
        qp = oq % 2
        ev_qt2 = [None, None]
        for tt in range(4):
            nb = tt % 2
            evw = S.step("g", ld(dqo[nb][:, :],
                                 wo_d[(oq * 4 + tt) * 128:(oq * 4 + tt + 1) * 128, :]),
                         waits=[dq_free[nb]], inc=g_dq[nb], n=16)
            dst = ap3(Ct, CW, woq_col[qp] + tt * 128, [[512, 16], [1, 128]])
            ev_qt2[nb] = S.step("s", lambda s_, nb=nb, dst=dst: s_.dma_start(
                out=dst, in_=dqo[nb][:, :], transpose=True),
                waits=[evw, woq_free[qp] if tt == 0 else None],
                inc=s_qt if nb == 0 else s_qtb, n=16)
            dq_free[nb] = ev_qt2[nb]
        ev_qt = ev_qt2
        order = list(range(1, 16)) + [0]
        for gi, ncl in enumerate(order):
            if oq == 0 and gi == 8:
                emit_col0()
            p = gi % 2

            def fo_(t, ncl=ncl, qp=qp, p=p):
                for j in range(CJ):
                    last = t.matmul(P[p][:, :],
                                    B[:, j * 2048 + ncl * 128:j * 2048 + ncl * 128 + 128],
                                    Ct[:, woq_col[qp] + j * 512:woq_col[qp] + (j + 1) * 512],
                                    start=(j == 0), stop=(j == CJ - 1))
                return last
            ev_g = S.step("t", fo_,
                          waits=ev_qt + [rwkv_last if ncl != 0 else col0_holder[0],
                                         ystg_free[p]], inc=st)
            last_out_group = ev_g
            ev_e = S.step("v", lambda v, p=p: v.tensor_copy(
                E[1][:, p * 512:(p + 1) * 512], P[p][:, :]), waits=[ev_g], inc=sv)
            out_evac_last[p] = ev_e
            ev_s = S.step("s", lambda s_, p=p, ncl=ncl, oq=oq: s_.dma_start(
                out=yscr_d[ncl * 128:(ncl + 1) * 128, oq * 512:(oq + 1) * 512],
                in_=E[1][:, p * 512:(p + 1) * 512]), waits=[ev_e], inc=s_y[p], n=16)
            ystg_free[p] = ev_s
            ystore[(oq, ncl)] = ev_s
        woq_free[qp] = last_out_group
        for _ in range(8):
            emit_prep_unit()
    while _pi[0] < len(prep_units):
        emit_prep_unit()
    last_ystore = tot(s_y[0], s_y[1])
    ev_prep_done = tot(s_stb[0], s_stb[1])

    S.step("g", ld(Ct[0:16, bupB_col:bupB_col + 8192], upB_d),
           waits=[last_out_group], inc=g_lo, n=16)
    S.step("g", ld(Ct[0:16, bdnB_col:bdnB_col + 2048], dnB_d), inc=g_lo, n=16)
    S.step("g", ld(Ct[:, adn_col:adn_col + 1024],
                   bass.AP(dnA_d.tensor, 0, [[16, 128], [2048, 64], [1, 16]]), slow=False),
           inc=g_lo, n=16)
    S.step("g", ld(Ct[:, aup_col:aup_col + 256],
                   bass.AP(upA_d.tensor, 0, [[16, 128], [2048, 16], [1, 16]])),
           inc=g_lo, n=16)
    ev_lora = S.step("g", ld(Ct[0:1, bdnv_col:bdnv_col + 2048],
                             bass.AP(bdn_d.tensor, 0, [[2048, 1], [1, 2048]])),
                     inc=g_lo, n=16)
    ev_ones = S.step("v", lambda v: v.memset(Ct[0:1, ones_col:ones_col + 128], 1.0),
                     waits=[last_out_group], inc=sv)

    def ln_chain(tile_t, slot, dump, waits_data):
        t = lnT[slot]
        sy_, sq_, mn_, vr_, nv_, sd_, rs_ = [x[:, :] for x in t]
        s4 = lnS4[slot]
        tile = tile_t[:, :]
        ev1 = S.step("v", lambda v: v.reduce_sum(sy_, tile, AX.X),
                     waits=list(waits_data), inc=sv)

        def fsq(v):
            last = None
            for i in range(4):
                last = v.scalar_tensor_tensor(
                    dump[:, :], tile_t[:, i * 512:(i + 1) * 512], 0.0,
                    tile_t[:, i * 512:(i + 1) * 512], ALU.add, ALU.mult,
                    accum_out=s4[:, i:i + 1])
            return last
        ev2 = S.step("v", fsq, waits=list(waits_data), inc=sv)
        S.step("v", lambda v: v.reduce_sum(sq_, s4[:, :], AX.X), waits=[ev2], inc=sv)
        S.step("v", lambda v: v.tensor_scalar(mn_, sy_, INV_C, None, ALU.mult),
               waits=[(sv, sv.n)], inc=sv)
        S.step("v", lambda v: v.tensor_scalar(vr_, sq_, INV_C, None, ALU.mult),
               waits=[(sv, sv.n)], inc=sv)
        ev5 = S.step("v", lambda v: v.scalar_tensor_tensor(nv_, mn_, mn_, vr_,
                                                           ALU.mult, ALU.subtract),
                     waits=[(sv, sv.n)], inc=sv)
        ev6 = S.step("c", lambda c_: c_.activation(sd_, nv_, AF.Sqrt,
                                                   bias=epst[:, :], scale=-1.0),
                     waits=[ev5], inc=sc)
        S.step("v", lambda v: v.reciprocal(rs_, sd_), waits=[ev6], inc=sv)
        ev8 = S.step("v", lambda v: v.tensor_scalar(tile, tile, mn_, rs_,
                                                    ALU.subtract, ALU.mult),
                     waits=[(sv, sv.n)], inc=sv)
        return ev8, ev2

    yln_free = [[], []]
    E_free = [None, None]
    hbf_free = [None, None]
    early_ul = {}
    hstore = [None] * NCK
    hT_done = [None] * NCK
    prep_all_done = (sv, sv.n)
    last_sq = None
    for ncl in range(NCK):
        b = ncl % 2
        ev_y = S.step("g", ld(yln[b][:, :], yscr_d[ncl * 128:(ncl + 1) * 128, :]),
                      waits=last_ystore + [prep_all_done] + yln_free[b],
                      inc=g_yl[b], n=16)
        ev_x = S.step("g", ld(E[b][:, :], x_d[ncl * 128:(ncl + 1) * 128, :]),
                      waits=[E_free[b]], inc=g_E[b], n=16)
        ev_add = S.step("v", lambda v, b=b: v.tensor_add(yln[b][:, :], yln[b][:, :],
                                                         E[b][:, :]),
                        waits=[ev_y, ev_x], inc=sv)
        E_free[b] = ev_add
        ev_ap, ev_sq = ln_chain(yln[b], b, P[4], [ev_add, last_sq])
        last_sq = ev_sq
        hstore[ncl] = S.step("s", lambda s_, b=b, ncl=ncl: s_.dma_start(
            out=hscr_d[ncl * 128:(ncl + 1) * 128, :], in_=yln[b][:, :]),
            waits=[ev_ap], inc=s_hst[b], n=16)
        ev_cv = S.step("v", lambda v, b=b: v.tensor_copy(
            Ct[:, b * 2048:(b + 1) * 2048], yln[b][:, :]),
            waits=[ev_ap, hbf_free[b]], inc=sv)
        hT_done[ncl] = S.step("s", lambda s_, b=b, ncl=ncl: s_.dma_start(
            out=A[:, :, ncl * 128:(ncl + 1) * 128],
            in_=Ct[:, b * 2048:(b + 1) * 2048], transpose=True),
            waits=[ev_cv], inc=s_hT[b], n=16)
        hbf_free[b] = hT_done[ncl]
        yln_free[b] = [hstore[ncl], ev_cv]
        if ncl == 2:
            for hcq in (0, 1):
                early_ul[hcq] = S.step(
                    "g", ld(dqo[hcq][:, :], upscr_d[hcq * 128:(hcq + 1) * 128, :]),
                    waits=list(ev_prep_done) + [dq_free[hcq]], inc=g_dq[hcq], n=16)
    ev_hT_all = tot(s_hT[0], s_hT[1])
    ev_hst_all = tot(s_hst[0], s_hst[1])

    slabL_free = [None, None]
    relu_done = [[None] * HCN for _ in range(4)]
    x2_reader = [None] * HCN
    pU = [P[0], P[1]]
    down_evac = {}
    down_sq = {}
    wdt_free = [None, None]
    bank_guard = {i: None for i in range(8)}
    bank_guard[4] = last_sq

    for s_ in range(4):
        def ftu(t, s_=s_):
            last = None
            for j in range(CJ):
                last = t.matmul(P[3][0:16, :],
                                Ct[:, aup_col + j * 16:aup_col + (j + 1) * 16],
                                Asl(j, s_ * 512, 512),
                                start=(j == 0), stop=(j == CJ - 1))
            return last
        wtu = [hT_done[4 * s_ + 2], hT_done[4 * s_ + 3], ev_lora]
        if s_ > 0:
            wtu.append(down_evac[(s_ - 1, 1, 0, 3)])
        ev_tu = S.step("t", ftu, waits=wtu, inc=st)
        ev_tue = S.step("v", lambda v: v.tensor_copy(Ct[0:16, tup_col:tup_col + 512],
                                                     P[3][0:16, :]),
                        waits=[ev_tu], inc=sv)
        ev_tdn_last = None
        for hc in range(HCN):
            b = hc % 2
            if s_ == 0 and hc < 2:
                ev_l = early_ul[hc]
            else:
                wl = list(ev_prep_done) + [slabL_free[b]]
                ev_l = S.step("g", ld(dqo[b][:, :], upscr_d[hc * 128:(hc + 1) * 128, :]),
                              waits=wl, inc=g_dq[b], n=16)

            def fu(t, b=b, hc=hc, s_=s_):
                for j in range(CJ):
                    t.matmul(pU[hc % 2][:, :], dqo[b][:, j * 128:(j + 1) * 128],
                             Asl(j, s_ * 512, 512), start=(j == 0), stop=False)
                return t.matmul(pU[hc % 2][:, :],
                                Ct[0:16, bupB_col + hc * 128:bupB_col + hc * 128 + 128],
                                Ct[0:16, tup_col:tup_col + 512],
                                start=False, stop=True)
            wu = [ev_l, ev_tue, hT_done[4 * s_ + 2], hT_done[4 * s_ + 3]]
            if hc >= 2:
                wu.append(relu_done[s_][hc - 2])
            if s_ > 0:
                wu.append(x2_reader[hc])
                if hc < 2:
                    wu.append(down_evac[(s_ - 1, 1, 0, hc % 2)])
            elif hc < 2:
                wu.append(out_evac_last[hc % 2])
            if bank_guard.get(hc % 2) is not None:
                wu.append(bank_guard[hc % 2])
            ev_u = S.step("t", fu, waits=wu, inc=st)
            slabL_free[b] = ev_u
            relu_done[s_][hc] = S.step("c", lambda c_, hc=hc: c_.activation(
                B[:, hc * 512:(hc + 1) * 512], pU[hc % 2][:, :], AF.Relu,
                bias=bup_t[:, hc:hc + 1]), waits=[ev_u], inc=sc)

            def emit_tdn(hcx):
                def ftd(t, hcx=hcx):
                    return t.matmul(P[2][0:16, :],
                                    Ct[:, adn_col + hcx * 16:adn_col + (hcx + 1) * 16],
                                    B[:, hcx * 512:(hcx + 1) * 512],
                                    start=(hcx == 0), stop=(hcx == HCN - 1))
                wtd = [relu_done[s_][hcx]]
                if hcx == 0:
                    if s_ > 0:
                        wtd.append(down_evac[(s_ - 1, 1, 0, 2)])
                    if bank_guard.get(2) is not None:
                        wtd.append(bank_guard[2])
                return S.step("t", ftd, waits=wtd, inc=st)
            if hc >= 1:
                emit_tdn(hc - 1)
            if hc == HCN - 1:
                ev_tdn_last = emit_tdn(hc)
        ev_tde = S.step("v", lambda v: v.tensor_copy(Ct[0:16, tdn_col:tdn_col + 512],
                                                     P[2][0:16, :]),
                        waits=[ev_tdn_last], inc=sv)
        for pair in range(2):
            ev_h = [None, None]
            for l in range(2):
                ncl = s_ * 4 + pair * 2 + l
                wE = list(ev_hst_all) + [E_free[l]]
                if pair == 1:
                    wE.append(down_evac[(s_, 0, l, 3)])
                elif s_ > 0:
                    wE.append(down_evac[(s_ - 1, 1, l, 3)])
                ev_h[l] = S.step("g", ld(E[l][:, :], hscr_d[ncl * 128:(ncl + 1) * 128, :]),
                                 waits=wE, inc=g_E[l], n=16)
            for hc in range(HCN):
                wb = hc % 2
                wwd = list(ev_prep_done) + [wdt_free[wb], last_out_group]
                if s_ == 0 and pair == 0 and hc < 2:
                    wwd += ev_hT_all
                ev_w = S.step("c",
                              lambda e_, wb=wb, hc=hc: e_.dma_start(
                    out=Ct[:, wdt_col[wb]:wdt_col[wb] + 2048],
                    in_=dnscr_d[0:2048, hc * 128:(hc + 1) * 128], transpose=True),
                    waits=wwd, inc=s_wd[wb], n=16)

                def fd(t, hc=hc, pair=pair, wb=wb):
                    last = None
                    for l in range(2):
                        for dt in range(4):
                            last = t.matmul(
                                P[l * 4 + dt][:, :],
                                B[:, hc * 512 + (pair * 2 + l) * 128:
                                  hc * 512 + (pair * 2 + l) * 128 + 128],
                                Ct[:, wdt_col[wb] + dt * 512:wdt_col[wb] + (dt + 1) * 512],
                                start=(hc == 0), stop=False)
                    return last
                wd = [ev_w, relu_done[s_][hc]]
                if hc == 0:
                    if pair == 0:
                        wd += [relu_done[s_][HCN - 1], ev_tde]
                        for bk in range(8):
                            if bank_guard.get(bk) is not None:
                                wd.append(bank_guard[bk])
                    else:
                        wd += [down_evac[(s_, 0, l2, dt2)]
                               for l2 in range(2) for dt2 in range(4)]
                        wd += [down_sq.get((s_, 0, 0)), down_sq.get((s_, 0, 1))]
                ev_d = S.step("t", fd, waits=wd, inc=st)
                wdt_free[wb] = ev_d
                if pair == 1:
                    x2_reader[hc] = ev_d
            grp_done = {}
            for l in range(2):
                for dt in range(4):
                    def ff(t, l=l, dt=dt, pair=pair):
                        t.matmul(P[l * 4 + dt][:, :],
                                 Ct[0:16, tdn_col + (pair * 2 + l) * 128:
                                    tdn_col + (pair * 2 + l) * 128 + 128],
                                 Ct[0:16, bdnB_col + dt * 512:bdnB_col + (dt + 1) * 512],
                                 start=False, stop=False)
                        return t.matmul(P[l * 4 + dt][:, :],
                                        Ct[0:1, ones_col:ones_col + 128],
                                        Ct[0:1, bdnv_col + dt * 512:bdnv_col + (dt + 1) * 512],
                                        start=False, stop=True)
                    grp_done[(l, dt)] = S.step("t", ff, waits=[ev_ones, ev_lora], inc=st)
            for l in range(2):
                ncl = s_ * 4 + pair * 2 + l
                for dt in range(4):
                    def fe(v, l=l, dt=dt):
                        return v.tensor_add(yln[l][:, dt * 512:(dt + 1) * 512],
                                            P[l * 4 + dt][:, :],
                                            E[l][:, dt * 512:(dt + 1) * 512])
                    down_evac[(s_, pair, l, dt)] = S.step(
                        "v", fe, waits=[grp_done[(l, dt)], ev_h[l]] + yln_free[l],
                        inc=sv)
                ev_ap, ev_sq = ln_chain(yln[l], l, P[l * 4],
                                        [down_evac[(s_, pair, l, 3)]])
                down_sq[(s_, pair, l)] = ev_sq
                bank_guard[l * 4] = ev_sq
                ev_so = S.step("s", lambda s2, l=l, ncl=ncl: s2.dma_start(
                    out=out_d[ncl * 128:(ncl + 1) * 128, :], in_=yln[l][:, :]),
                    waits=[ev_ap], inc=s_out[l], n=16)
                yln_free[l] = [ev_so]
                E_free[l] = down_evac[(s_, pair, l, 3)]

    dma_sems = ([g_par, g_wo, g_lo] + g_dq + g_qi + g_sc + g_nw + g_E + g_yl
                + s_x + s_sl + s_ut + s_stb + [s_qt] + s_y + s_out + s_hst
                + s_hT + s_wd)
    S.step("s", None, waits=tot(*dma_sems))
    S.step("g", None, waits=tot(s_out[0], s_out[1]))

    S.check_deadlock()

    with es, nc.Block() as block:
        def runner(engname):
            def run(eng):
                for w, fn, inc, n in S.streams[engname]:
                    for sem, val in w:
                        eng.wait_ge(sem.h, val)
                    if fn is None:
                        continue
                    last = fn(eng)
                    if inc is not None:
                        last.then_inc(inc.h, n)
            return run
        block.gpsimd(runner("g"))
        block.sync(runner("s"))
        block.tensor(runner("t"))
        block.vector(runner("v"))
        block.scalar(runner("c"))
    return nc


def kernel(**inputs):
    x = np.asarray(inputs["L0_attn_x"])

    def vec(name):
        return np.ascontiguousarray(np.asarray(inputs[name]).reshape(-1))
    common = {
        "w_rec": np.ascontiguousarray(inputs["L0_attn_w_rec"]),
        "w_value": np.ascontiguousarray(inputs["L0_attn_w_value"]),
        "w_key": np.ascontiguousarray(inputs["L0_attn_w_key"]),
        "w_out": np.ascontiguousarray(inputs["L0_attn_w_out"]),
        "tf": vec("L0_attn_time_first"),
        "pp": vec("L0_attn_pp"),
        "aa": vec("L0_attn_aa"),
        "bb": vec("L0_attn_bb"),
        "tmr": vec("L0_attn_time_mix_r"),
        "tmv": vec("L0_attn_time_mix_v"),
        "tmk": vec("L0_attn_time_mix_k"),
        "xx": vec("L0_attn_xx"),
        "w_up_q": np.ascontiguousarray(inputs["L0_ffn_w_up_q"]),
        "w_up_scale": np.ascontiguousarray(inputs["L0_ffn_w_up_scale"]),
        "b_up": vec("L0_ffn_b_up"),
        "up_lora_a": np.ascontiguousarray(inputs["L0_ffn_w_up_lora_a"]),
        "up_lora_b": np.ascontiguousarray(inputs["L0_ffn_w_up_lora_b"]),
        "w_down_q": np.ascontiguousarray(inputs["L0_ffn_w_down_q"]),
        "w_down_scale": np.ascontiguousarray(inputs["L0_ffn_w_down_scale"]),
        "b_down": vec("L0_ffn_b_down"),
        "dn_lora_a": np.ascontiguousarray(inputs["L0_ffn_w_down_lora_a"]),
        "dn_lora_b": np.ascontiguousarray(inputs["L0_ffn_w_down_lora_b"]),
    }
    nc = build()
    in_maps = [dict(common, x=np.ascontiguousarray(x[i])) for i in range(8)]
    res = run_bass_kernel_spmd(nc, in_maps, core_ids=list(range(8)))
    return np.stack([np.asarray(r["out"], dtype=np.float32) for r in res.results],
                    axis=0)
